# version 07
# baseline (speedup 1.0000x reference)
import contextlib
import numpy as np
import concourse.bass as bass
import concourse.mybir as mybir
from concourse.bass_utils import run_bass_kernel_spmd

F32 = mybir.dt.float32
BF16 = mybir.dt.bfloat16
I8 = mybir.dt.int8
AF = mybir.ActivationFunctionType
ALU = mybir.AluOpType

D = 2048
NT = 16
SEQ = 4096
RK = 512
NCH = SEQ // RK
RF = 256
C_Q = float(128 ** 3)
C_KV = 128.0

_EK = ("t", "v", "s", "g", "q")


class Bld:
    def __init__(self, nc, block, semnames, es):
        self.nc = nc
        self.sec = {"t": block.tensor, "v": block.vector, "s": block.scalar,
                    "g": block.gpsimd, "q": block.sync}
        self.sem, self.cnt = {}, {}
        self.seen = {k: {} for k in _EK}
        for n in semnames:
            self.sem[n] = es.enter_context(nc.semaphore(n))
            self.cnt[n] = 0
        self.wr, self.rd = {}, {}
        self.pend = {}

    def emit(self, ekey, deps, fn, inc):
        step = 16 if inc.startswith("D") else 1
        todo, seen = [], self.seen[ekey]
        for d in deps:
            if d is None:
                continue
            s, v = d
            if v > 0 and seen.get(s, 0) < v:
                todo.append((s, v))
                seen[s] = v
        semh, inc_sem = self.sem, self.sem[inc]

        def body(eng):
            for s, v in todo:
                eng.wait_ge(semh[s], v)
            fn(eng).then_inc(inc_sem, step)

        self.pend.setdefault(ekey, []).append(body)
        self.cnt[inc] += step
        return (inc, self.cnt[inc])

    def flush(self):
        for ekey, bodies in self.pend.items():
            def run_all(eng, bodies_=bodies):
                for b in bodies_:
                    b(eng)
            self.sec[ekey](run_all)
        self.pend = {}

    def dep_r(self, *bufs):
        return [self.wr.get(b) for b in bufs]

    def dep_w(self, *bufs):
        out = []
        for b in bufs:
            out.extend(self.rd.get(b, {}).items())
            if b in self.wr:
                out.append(self.wr[b])
        return out

    def did_r(self, mark, *bufs):
        for b in bufs:
            d = self.rd.setdefault(b, {})
            if d.get(mark[0], 0) < mark[1]:
                d[mark[0]] = mark[1]

    def did_w(self, mark, *bufs):
        for b in bufs:
            self.wr[b] = mark
            self.rd.pop(b, None)


def build_program():
    nc = bass.Bass("TRN2", target_bir_lowering=False)
    dti = lambda n, s, d: nc.dram_tensor(n, s, d, kind="ExternalInput").ap()
    xkv = dti("xkv", [D, SEQ], F32)
    wq = dti("wq", [16, 128, 16, 128], F32)
    wkv = dti("wkv", [16, 128, 16, 128], F32)
    wo = dti("wo", [16, 128, 16, 128], F32)
    wg = dti("wg", [64, 128, 16, 128], I8)
    wu = dti("wu", [64, 128, 16, 128], I8)
    wd = dti("wd", [16, 128, 64, 128], I8)
    sg = dti("sg", [128, 64], F32)
    su = dti("su", [128, 64], F32)
    sd = dti("sd", [128, 16], F32)
    cfm = dti("cf", [128, 128], F32)
    sfm = dti("sf", [128, 128], F32)
    mask = dti("mask", [16, SEQ], F32)
    out = nc.dram_tensor("out", [D, 1024], F32, kind="ExternalOutput").ap()
    nscr = nc.dram_tensor("nscr", [2, 16, RK], BF16, kind="Internal").ap()
    rnscr = nc.dram_tensor("rnscr", [2, 16, RK], BF16, kind="Internal").ap()
    ascr = nc.dram_tensor("ascr", [D, RK], BF16, kind="Internal").ap()

    sems = ["T", "V", "S", "G", "DC",
            "DX0", "DX1", "DW0", "DW1", "DG0", "DG1", "DU0", "DU1",
            "DD0", "DD1", "DN0", "DN1", "DR0", "DR1", "DA0", "DA1",
            "DB0", "DB1", "DM0", "DM1", "DO"]

    es = contextlib.ExitStack()
    with es:
        sb = lambda n, s, d: es.enter_context(nc.sbuf_tensor(n, s, d))
        xn_roll = sb("xn_roll", [128, NT * RK], BF16)
        xn_own = sb("xn_own", [128, NT * RK], BF16)
        kvb = sb("kvb", [128, NT * RK], BF16)
        carry = [sb("carry0", [128, NT], F32), sb("carry1", [128, NT], F32)]
        ctmp = sb("ctmp", [128, NT], F32)
        sqt = [sb("sqt0", [128, RK], BF16), sb("sqt1", [128, RK], BF16)]
        nbuf = [sb("nbuf0", [1, RK], BF16), sb("nbuf1", [1, RK], BF16)]
        n16 = sb("n16", [16, RK], BF16)
        rn32 = sb("rn32", [16, RK], F32)
        rnm = sb("rnm", [16, RK], BF16)
        rnb = [sb("rnb0", [128, RK], BF16), sb("rnb1", [128, RK], BF16)]
        mv16 = sb("mv16", [1, RK], BF16)
        e2f = sb("e2f", [1, RK], BF16)
        vtmp = sb("vtmp", [1, RK], BF16)
        rstdf = sb("rstdf", [1, RK], F32)
        rstd16 = sb("rstd16", [1, RK], BF16)
        qsb = sb("qsb", [128, NT * RK], BF16)
        vh = sb("vh", [128, NT * RK], BF16)
        hb = [[sb(f"hb{p}_{i}", [128, RK], BF16) for i in range(6)] for p in range(2)]
        attn_st = [sb("ast0", [128, RK], BF16)]
        wbuf = [sb("wb0", [128, 2048], BF16), sb("wb1", [128, 2048], BF16)]
        wgb = [sb("wgb0", [128, 2048], BF16), sb("wgb1", [128, 2048], BF16)]
        wub = [sb("wub0", [128, 2048], BF16), sb("wub1", [128, 2048], BF16)]
        wdb = [sb("wdb0", [128, 2048], BF16), sb("wdb1", [128, 2048], BF16)]
        fuse = sb("fuse", [128, 16 * RK], BF16)
        silg = [sb("silg0", [128, RK], BF16), sb("silg1", [128, RK], BF16)]
        sgv = sb("sgv", [128, 64], F32)
        suv = sb("suv", [128, 64], F32)
        sdv = sb("sdv", [128, 16], F32)
        cfb = sb("cfb", [128, 128], BF16)
        sfb = sb("sfb", [128, 128], BF16)
        onesM = sb("onesM", [128, 1], BF16)
        ones1 = sb("ones1", [1, 128], BF16)
        onesS = sb("onesS", [128, RK], BF16)
        maskb = [sb("mkb0", [16, RK], BF16), sb("mkb1", [16, RK], BF16)]
        a_rd = [sb("ard0", [128, RK], BF16), sb("ard1", [128, RK], BF16)]
        x_rd = [sb("xrd0", [128, RK], BF16), sb("xrd1", [128, RK], BF16)]
        axr = [sb("axr0", [128, RK], F32), sb("axr1", [128, RK], F32)]
        outst = [sb("ost0", [128, RK], F32), sb("ost1", [128, RK], F32)]
        acc = sb("acc", [128, NT * RK], F32)
        P = [es.enter_context(nc.psum_tensor(f"P{i}", [128, 512], F32)) for i in range(8)]
        sbuf_left = nc.sbuf_bytes_remaining
        print('SBUF remaining after alloc:', sbuf_left)
        block = es.enter_context(nc.Block())
        B = Bld(nc, block, sems, es)
        B.sbuf_left = sbuf_left

        def prolog(g):
            g.memset(onesM[:], 1.0)
            g.memset(ones1[:], 1.0)
            g.memset(onesS[:], 1.0)
            g.memset(carry[0][:], 0.0)
            return g.memset(carry[1][:], 0.0)
        mk = B.emit("g", [], prolog, "G")
        B.did_w(mk, "onesM", "ones1", "onesS", "carry0", "carry1")
        for dst, src, nm in ((sgv, sg, "sgv"), (suv, su, "suv"), (sdv, sd, "sdv"),
                             (cfb, cfm, "cfb"), (sfb, sfm, "sfb")):
            mk = B.emit("g", [], (lambda d_, s_: (lambda g: g.dma_start(out=d_[:], in_=s_)))(dst, src), "DC")
            B.did_w(mk, nm)

        def dma_x_chunk(c, dst, xname):
            sem = "DX%d" % (c % 2)
            mk = B.emit("g", B.dep_w(xname),
                        lambda g, d_=dst, c_=c: g.dma_start(
                            out=d_[:, 0:NT * RK].rearrange("p (t r) -> p t r", t=NT),
                            in_=xkv[:, c_ * RK:(c_ + 1) * RK].rearrange("(t p) r -> p t r", p=128)),
                        sem)
            B.did_w(mk, xname, *[xname + "_t%d" % i for i in range(NT)])
            return mk

        def dma_mask(c):
            p = c % 2
            mk = B.emit("g", B.dep_w("mkb%d" % p),
                        lambda g, c_=c, p_=p: g.dma_start(out=maskb[p_][:], in_=mask[:, c_ * RK:(c_ + 1) * RK]),
                        "DM%d" % p)
            B.did_w(mk, "mkb%d" % p)

        def dma_wtile(wsrc, mt, bufs, bname, sem_base):
            par = mt % 2
            nm = bname + str(par)
            mk = B.emit("g", B.dep_w(nm),
                        lambda g, w_=wsrc, m_=mt, d_=bufs[par]: g.dma_start(
                            out=d_[:], in_=w_[m_].rearrange("p a b -> p (a b)")),
                        "%s%d" % (sem_base, par))
            B.did_w(mk, nm)

        def dma_wd_piece(ot, grp, par):
            nm = "wdb%d" % par
            mk = B.emit("g", B.dep_w(nm),
                        lambda g, o_=ot, p_=grp, d_=wdb[par]: g.dma_start(
                            out=d_[:], in_=wd[o_, :, p_ * 16:(p_ + 1) * 16, :].rearrange("p a b -> p (a b)")),
                        "DD%d" % par)
            B.did_w(mk, nm)

        def ln_chunk(c, xdst, xname, own):
            def mm_sum(t, x_=xdst):
                r = None
                for kt in range(NT):
                    r = t.matmul(P[2][0:1, :], onesM[:], x_[:, kt * RK:(kt + 1) * RK],
                                 start=(kt == 0), stop=(kt == NT - 1))
                return r
            mk = B.emit("t", B.dep_r(xname, "onesM") + B.dep_w("P2"), mm_sum, "T")
            B.did_w(mk, "P2"); B.did_r(mk, xname, "onesM")
            mk = B.emit("v", B.dep_r("P2") + B.dep_w("mv16"),
                        lambda v: v.tensor_scalar_mul(mv16[0:1, :], P[2][0:1, :], 1.0 / D), "V")
            B.did_w(mk, "mv16"); B.did_r(mk, "P2")
            if own:
                for kt in range(NT):
                    p = kt % 2
                    mk = B.emit("v", B.dep_r(xname) + B.dep_w("sqt%d" % p),
                                lambda v, k_=kt, p_=p, x_=xdst: v.tensor_tensor(
                                    sqt[p_][:], x_[:, k_ * RK:(k_ + 1) * RK],
                                    x_[:, k_ * RK:(k_ + 1) * RK], op=ALU.mult), "V")
                    B.did_w(mk, "sqt%d" % p); B.did_r(mk, xname)
                    mk2 = B.emit("t", B.dep_r("sqt%d" % p, "onesM") + (B.dep_w("P3") if kt == 0 else []),
                                 lambda t, k_=kt, p_=p: t.matmul(P[3][0:1, :], onesM[:], sqt[p_][:],
                                                                 start=(k_ == 0), stop=(k_ == NT - 1)), "T")
                    B.did_r(mk2, "sqt%d" % p, "onesM")
                    if kt == NT - 1:
                        B.did_w(mk2, "P3")
                mk = B.emit("v", B.dep_r("P3") + B.dep_w("e2f"),
                            lambda v: v.tensor_scalar_mul(e2f[0:1, :], P[3][0:1, :], 1.0 / D), "V")
                B.did_w(mk, "e2f"); B.did_r(mk, "P3")
                mk = B.emit("v", B.dep_r("mv16") + B.dep_w("vtmp"),
                            lambda v: v.tensor_tensor(vtmp[0:1, :], mv16[0:1, :], mv16[0:1, :], op=ALU.mult), "V")
                B.did_w(mk, "vtmp"); B.did_r(mk, "mv16")
                mk = B.emit("v", B.dep_r("e2f") + B.dep_w("vtmp"),
                            lambda v: v.tensor_tensor(vtmp[0:1, :], e2f[0:1, :], vtmp[0:1, :], op=ALU.subtract), "V")
                B.did_w(mk, "vtmp"); B.did_r(mk, "e2f")
                mk = B.emit("v", B.dep_r("vtmp"),
                            lambda v: v.tensor_scalar_add(vtmp[0:1, :], vtmp[0:1, :], 1e-5), "V")
                B.did_w(mk, "vtmp")
                mk = B.emit("s", B.dep_r("vtmp") + B.dep_w("rstdf"),
                            lambda s: s.activation(rstdf[0:1, :], vtmp[0:1, :], AF.Sqrt), "S")
                B.did_w(mk, "rstdf"); B.did_r(mk, "vtmp")
                mk = B.emit("v", B.dep_r("rstdf"),
                            lambda v: v.reciprocal(rstdf[0:1, :], rstdf[0:1, :]), "V")
                B.did_w(mk, "rstdf")
                mk = B.emit("v", B.dep_r("rstdf") + B.dep_w("rstd16"),
                            lambda v: v.tensor_copy(rstd16[0:1, :], rstdf[0:1, :]), "V")
                B.did_w(mk, "rstd16"); B.did_r(mk, "rstdf")
            mk = B.emit("t", B.dep_r("mv16", "ones1") + B.dep_w("P4"),
                        lambda t: t.matmul(P[4][:], ones1[:], mv16[0:1, :], start=True, stop=True), "T")
            B.did_w(mk, "P4"); B.did_r(mk, "mv16", "ones1")
            if own:
                mk = B.emit("t", B.dep_r("rstd16", "ones1") + B.dep_w("P5"),
                            lambda t: t.matmul(P[5][:], ones1[:], rstd16[0:1, :], start=True, stop=True), "T")
                B.did_w(mk, "P5"); B.did_r(mk, "rstd16", "ones1")
            for kt in range(NT):
                mk = B.emit("v", B.dep_w(xname) + B.dep_r("P4"),
                            lambda v, k_=kt, x_=xdst: v.tensor_tensor(
                                x_[:, k_ * RK:(k_ + 1) * RK], x_[:, k_ * RK:(k_ + 1) * RK],
                                P[4][:], op=ALU.subtract), "V")
                B.did_w(mk, xname) if kt == NT - 1 else B.did_r(mk, xname)
                B.did_r(mk, "P4")
                if own:
                    mk = B.emit("v", B.dep_r("P5"),
                                lambda v, k_=kt, x_=xdst: v.tensor_tensor(
                                    x_[:, k_ * RK:(k_ + 1) * RK], x_[:, k_ * RK:(k_ + 1) * RK],
                                    P[5][:], op=ALU.mult), "V")
                    B.did_w(mk, xname) if kt == NT - 1 else B.did_r(mk, xname)
                    B.did_r(mk, "P5")
            for i in range(NT):
                B.did_w(B.wr[xname], xname + "_t%d" % i)

        def proj(xsrc, xname, wsrc, dst, dstname, psA, psB):
            dma_wtile(wsrc, 0, wbuf, "wb", "DW")
            dma_wtile(wsrc, 1, wbuf, "wb", "DW")
            pps = [psA, psB]
            for mt in range(NT):
                par = mt % 2
                pp = P[pps[par]]
                pn = "P%d" % pps[par]
                def mm(t, p_=pp, w_=wbuf[par], x_=xsrc):
                    r = None
                    for kt in range(NT):
                        r = t.matmul(p_[:], w_[:, kt * 128:(kt + 1) * 128],
                                     x_[:, kt * RK:(kt + 1) * RK],
                                     start=(kt == 0), stop=(kt == NT - 1))
                    return r
                mk = B.emit("t", B.dep_r("wb%d" % par, xname) + B.dep_w(pn), mm, "T")
                B.did_w(mk, pn); B.did_r(mk, "wb%d" % par, xname)
                if mt + 2 < NT:
                    dma_wtile(wsrc, mt + 2, wbuf, "wb", "DW")
                mk = B.emit("s", B.dep_r(pn) + B.dep_w(dstname + "_t%d" % mt),
                            lambda s, m_=mt, p_=pp, d_=dst: s.copy(d_[:, m_ * RK:(m_ + 1) * RK], p_[:]), "S")
                B.did_r(mk, pn); B.did_w(mk, dstname + "_t%d" % mt)

        def norms_front(srcbuf, srcname, cpar):
            nsem = "DN%d" % cpar
            for mt in range(NT):
                p = mt % 2
                mk = B.emit("v", B.dep_r(srcname + "_t%d" % mt) + B.dep_w("sqt%d" % p),
                            lambda v, m_=mt, p_=p, x_=srcbuf: v.tensor_tensor(
                                sqt[p_][:], x_[:, m_ * RK:(m_ + 1) * RK],
                                x_[:, m_ * RK:(m_ + 1) * RK], op=ALU.mult), "V")
                B.did_w(mk, "sqt%d" % p); B.did_r(mk, srcname + "_t%d" % mt)
                pn = "P%d" % (6 + p)
                mk = B.emit("t", B.dep_r("sqt%d" % p, "onesM") + B.dep_w(pn),
                            lambda t, p_=p: t.matmul(P[6 + p_][0:1, :], onesM[:], sqt[p_][:],
                                                     start=True, stop=True), "T")
                B.did_w(mk, pn); B.did_r(mk, "sqt%d" % p, "onesM")
                mk = B.emit("s", B.dep_r(pn) + B.dep_w("nbuf%d" % p),
                            lambda s, p_=p: s.copy(nbuf[p_][0:1, :], P[6 + p_][0:1, :]), "S")
                B.did_r(mk, pn); B.did_w(mk, "nbuf%d" % p)
                mk = B.emit("q", B.dep_r("nbuf%d" % p) + B.dep_w("nscr%d" % cpar),
                            lambda q, m_=mt, p_=p, c_=cpar: q.dma_start(
                                out=nscr[c_, m_].unsqueeze(0), in_=nbuf[p_][0:1, :]), nsem)
                B.did_r(mk, "nbuf%d" % p)
                if mt == NT - 1:
                    B.did_w(mk, "nscr%d" % cpar)
                else:
                    B.did_w(mk, "nscr_part%d" % cpar)

        def norms_back(srcbuf, srcname, cpar, sq_scale, maskname):
            nsem = "DN%d" % cpar
            rsem = "DR%d" % cpar
            mk = B.emit("q", [(nsem, B.cnt[nsem])] + B.dep_w("n16"),
                        lambda q, c_=cpar: q.dma_start(out=n16[:], in_=nscr[c_]), nsem)
            B.did_w(mk, "n16"); B.did_r(mk, "nscr%d" % cpar)
            mk = B.emit("s", B.dep_r("n16") + B.dep_w("rn32"),
                        lambda s: s.activation(rn32[:], n16[:], AF.Sqrt, scale=float(sq_scale)), "S")
            B.did_w(mk, "rn32"); B.did_r(mk, "n16")
            mk = B.emit("v", B.dep_w("rn32"),
                        lambda v: v.reciprocal(rn32[:], rn32[:]), "V")
            B.did_w(mk, "rn32")
            if maskname is not None:
                mi = int(maskname[-1])
                mk = B.emit("v", B.dep_r("rn32", maskname) + B.dep_w("rnm"),
                            lambda v, m_=mi: v.tensor_tensor(rnm[:], rn32[:], maskb[m_][:], op=ALU.mult), "V")
                B.did_r(mk, "rn32", maskname)
            else:
                mk = B.emit("v", B.dep_r("rn32") + B.dep_w("rnm"),
                            lambda v: v.tensor_copy(rnm[:], rn32[:]), "V")
                B.did_r(mk, "rn32")
            B.did_w(mk, "rnm")
            mk = B.emit("q", B.dep_r("rnm") + B.dep_w("rnscr%d" % cpar),
                        lambda q, c_=cpar: q.dma_start(out=rnscr[c_], in_=rnm[:]), rsem)
            B.did_r(mk, "rnm"); B.did_w(mk, "rnscr%d" % cpar)
            for mt in range(NT):
                p = mt % 2
                mk = B.emit("g", B.dep_r("rnscr%d" % cpar) + B.dep_w("rnb%d" % p),
                            lambda g, m_=mt, p_=p, c_=cpar: g.dma_start(
                                out=rnb[p_][:], in_=rnscr[c_, m_].unsqueeze(0).broadcast_to((128, RK))), rsem)
                B.did_w(mk, "rnb%d" % p); B.did_r(mk, "rnscr%d" % cpar)
                mk = B.emit("v", B.dep_r("rnb%d" % p) + B.dep_w(srcname + "_t%d" % mt),
                            lambda v, m_=mt, p_=p, x_=srcbuf: v.tensor_tensor(
                                x_[:, m_ * RK:(m_ + 1) * RK], x_[:, m_ * RK:(m_ + 1) * RK],
                                rnb[p_][:], op=ALU.mult), "V")
                B.did_w(mk, srcname + "_t%d" % mt); B.did_r(mk, "rnb%d" % p)

        def xslot(c):
            if c >= 6:
                return xn_own, "xn_own"
            return (xn_roll, "xn_roll") if c % 2 == 0 else (qsb, "qsb")

        def kvslot(c):
            if c >= 6 or c % 2 == 0:
                return kvb, "kvb"
            return vh, "vh"

        def k_front(c):
            own = c >= 6
            xdst, xname = xslot(c)
            kdst, kn = kvslot(c)
            dma_mask(c)
            ln_chunk(c, xdst, xname, own)
            proj(xdst, xname, wkv, kdst, kn, 0, 1)
            if c + 2 <= 6:
                dma_x_chunk(c + 2, *xslot(c + 2))

        def k_tail(c):
            cpar = c % 2
            kdst, kn = kvslot(c)
            norms_front(kdst, kn, cpar)
            norms_back(kdst, kn, cpar, C_KV, "mkb%d" % cpar)
            cold, cnew = carry[c % 2], carry[1 - c % 2]
            coldn, cnewn = "carry%d" % (c % 2), "carry%d" % (1 - c % 2)
            for mt in range(NT):
                p = mt % 2
                mk = B.emit("s", B.dep_r(kn + "_t%d" % mt) + B.dep_w("ctmp", "sqt%d" % p),
                            lambda s, m_=mt, p_=p, x_=kdst: s.activation(
                                sqt[p_][:], x_[:, m_ * RK:(m_ + 1) * RK], AF.Copy,
                                accum_out=ctmp[:, m_:m_ + 1]), "S")
                B.did_w(mk, "ctmp", "sqt%d" % p); B.did_r(mk, kn + "_t%d" % mt)
                mk = B.emit("g", B.dep_r("ctmp", coldn) + (B.dep_w(cnewn) if mt == 0 else []),
                            lambda g, m_=mt, co_=cold, cn_=cnew: g.tensor_tensor(
                                cn_[:, m_:m_ + 1], co_[:, m_:m_ + 1], ctmp[:, m_:m_ + 1], op=ALU.add), "G")
                B.did_r(mk, "ctmp", coldn)
                if mt == NT - 1:
                    B.did_w(mk, cnewn)

        def own_scans(c):
            cold = carry[c % 2]
            coldn = "carry%d" % (c % 2)
            for mt in range(NT):
                mk = B.emit("v", B.dep_w("kvb_t%d" % mt) + B.dep_r(coldn, "onesS"),
                            lambda v, m_=mt, co_=cold: v.tensor_tensor_scan(
                                kvb[:, m_ * RK:(m_ + 1) * RK], onesS[:],
                                kvb[:, m_ * RK:(m_ + 1) * RK], co_[:, m_:m_ + 1],
                                op0=ALU.mult, op1=ALU.add), "V")
                B.did_w(mk, "kvb_t%d" % mt)
                B.did_r(mk, coldn, "onesS")

        def q_chunk(oc):
            cpar = oc
            kname = "kvb"
            proj(xn_own, "xn_own", wq, qsb, "qsb", 0, 1)
            norms_front(qsb, "qsb", cpar)
            own_scans(6 + oc)
            norms_back(qsb, "qsb", cpar, C_Q, None)
            for mt in range(NT):
                p = mt % 2
                names = [f"hb{p}_{i}" for i in range(6)]
                pa, pb = P[2 * p], P[2 * p + 1]
                pan, pbn = "P%d" % (2 * p), "P%d" % (2 * p + 1)
                for (src, srcn, mat, matn, dsti) in (
                        (qsb, "qsb_t%d" % mt, cfb, "cfb", 0),
                        (qsb, "qsb_t%d" % mt, sfb, "sfb", 1),
                        (kvb, kname + "_t%d" % mt, cfb, "cfb", 2),
                        (kvb, kname + "_t%d" % mt, sfb, "sfb", 3)):
                    pp, pn = (pa, pan) if dsti % 2 == 0 else (pb, pbn)
                    mk = B.emit("t", B.dep_r(srcn, matn) + B.dep_w(pn),
                                lambda t, s_=src, m_=mt, w_=mat, p_=pp: t.matmul(
                                    p_[:], w_[:], s_[:, m_ * RK:(m_ + 1) * RK], start=True, stop=True), "T")
                    B.did_w(mk, pn); B.did_r(mk, srcn, matn)
                    mk = B.emit("s", B.dep_r(pn) + B.dep_w(names[dsti]),
                                lambda s, b_=hb[p][dsti], p_=pp: s.copy(b_[:], p_[:]), "S")
                    B.did_r(mk, pn); B.did_w(mk, names[dsti])
                def prods(v, b_=hb[p]):
                    QC, QS, KC, KS, T5, T6 = b_
                    v.tensor_tensor(T5[:], QC[:], KC[:], op=ALU.mult)
                    v.tensor_tensor(T6[:], QS[:], KS[:], op=ALU.mult)
                    v.tensor_tensor(T5[:], T5[:], T6[:], op=ALU.subtract)
                    v.tensor_tensor(T6[:], QC[:], KS[:], op=ALU.mult)
                    v.tensor_tensor(QC[:], QS[:], KC[:], op=ALU.mult)
                    return v.tensor_tensor(T6[:], T6[:], QC[:], op=ALU.add)
                mk = B.emit("v", B.dep_r(*names) + B.dep_w(names[0], names[4], names[5]), prods, "V")
                B.did_r(mk, *names); B.did_w(mk, names[0], names[4], names[5])
                mk = B.emit("t", B.dep_r(names[4], names[5], "cfb", "sfb") + B.dep_w("P4"),
                            lambda t, b_=hb[p]: (t.matmul(P[4][:], cfb[:], b_[4][:], start=True, stop=False),
                                                 t.matmul(P[4][:], sfb[:], b_[5][:], start=False, stop=True))[-1], "T")
                B.did_w(mk, "P4"); B.did_r(mk, names[4], names[5], "cfb", "sfb")
                mk = B.emit("s", B.dep_r("P4") + B.dep_w("vh_t%d" % mt),
                            lambda s, m_=mt: s.copy(vh[:, m_ * RK:(m_ + 1) * RK], P[4][:]), "S")
                B.did_r(mk, "P4"); B.did_w(mk, "vh_t%d" % mt)
            vh_names = ["vh_t%d" % k for k in range(NT)]
            dma_wtile(wo, 0, wbuf, "wb", "DW")
            dma_wtile(wo, 1, wbuf, "wb", "DW")
            for mt in range(NT):
                par = mt % 2
                pp, pn = P[6 + par], "P%d" % (6 + par)
                def mmo(t, p_=pp, w_=wbuf[par]):
                    r = None
                    for kt in range(NT):
                        r = t.matmul(p_[:], w_[:, kt * 128:(kt + 1) * 128],
                                     vh[:, kt * RK:(kt + 1) * RK], start=(kt == 0), stop=(kt == NT - 1))
                    return r
                mk = B.emit("t", B.dep_r("wb%d" % par, *vh_names) + B.dep_w(pn), mmo, "T")
                B.did_w(mk, pn); B.did_r(mk, "wb%d" % par, *vh_names)
                if mt + 2 < NT:
                    dma_wtile(wo, mt + 2, wbuf, "wb", "DW")
                mk = B.emit("s", B.dep_r(pn) + B.dep_w("ast0"),
                            lambda s, p_=pp, a_=attn_st[0]: s.copy(a_[:], p_[:]), "S")
                B.did_r(mk, pn); B.did_w(mk, "ast0")
                mk = B.emit("q", B.dep_r("ast0") + B.dep_w("ascr_t%d" % mt),
                            lambda q, m_=mt, a_=attn_st[0]: q.dma_start(
                                out=ascr[m_ * 128:(m_ + 1) * 128, :], in_=a_[:]), "DA%d" % par)
                B.did_r(mk, "ast0"); B.did_w(mk, "ascr_t%d" % mt)

        def f_chunk(oc):
            outcol = oc * RK
            dma_wtile(wg, 0, wgb, "wgb", "DG")
            dma_wtile(wu, 0, wub, "wub", "DU")
            dma_wtile(wg, 1, wgb, "wgb", "DG")
            dma_wtile(wu, 1, wub, "wub", "DU")
            for grp in range(4):
                for i in range(16):
                    mt = grp * 16 + i
                    par = mt % 2
                    pg, pgn = P[par], "P%d" % par
                    pu, pun = P[2 + par], "P%d" % (2 + par)
                    def mmg(t, p_=pg, w_=wgb[par]):
                        r = None
                        for kt in range(NT):
                            r = t.matmul(p_[:], w_[:, kt * 128:(kt + 1) * 128],
                                         xn_own[:, kt * RK:(kt + 1) * RK],
                                         start=(kt == 0), stop=(kt == NT - 1))
                        return r
                    mk = B.emit("t", B.dep_r("wgb%d" % par, "xn_own") + B.dep_w(pgn), mmg, "T")
                    B.did_w(mk, pgn); B.did_r(mk, "wgb%d" % par, "xn_own")
                    if mt + 2 < 64:
                        dma_wtile(wg, mt + 2, wgb, "wgb", "DG")
                    def mmu(t, p_=pu, w_=wub[par]):
                        r = None
                        for kt in range(NT):
                            r = t.matmul(p_[:], w_[:, kt * 128:(kt + 1) * 128],
                                         xn_own[:, kt * RK:(kt + 1) * RK],
                                         start=(kt == 0), stop=(kt == NT - 1))
                        return r
                    mk = B.emit("t", B.dep_r("wub%d" % par, "xn_own") + B.dep_w(pun), mmu, "T")
                    B.did_w(mk, pun); B.did_r(mk, "wub%d" % par, "xn_own")
                    if mt + 2 < 64:
                        dma_wtile(wu, mt + 2, wub, "wub", "DU")
                    mk = B.emit("s", B.dep_r(pgn, "sgv") + B.dep_w("silg%d" % par),
                                lambda s, m_=mt, p_=pg, g_=silg[par]: s.activation(
                                    g_[:], p_[:], AF.Silu, scale=sgv[:, m_:m_ + 1]), "S")
                    B.did_r(mk, pgn, "sgv"); B.did_w(mk, "silg%d" % par)
                    mk = B.emit("v", B.dep_r(pun, "suv", "silg%d" % par) + B.dep_w("fuse_t%d" % i),
                                lambda v, m_=mt, i_=i, p_=pu, g_=silg[par]: v.scalar_tensor_tensor(
                                    fuse[:, i_ * RK:(i_ + 1) * RK], p_[:], suv[:, m_:m_ + 1], g_[:],
                                    op0=ALU.mult, op1=ALU.mult), "V")
                    B.did_r(mk, pun, "suv", "silg%d" % par); B.did_w(mk, "fuse_t%d" % i)
                if grp == 0:
                    dma_wd_piece(0, grp, 0)
                    dma_wd_piece(1, grp, 1)
                for ot in range(NT):
                    dpar = ot % 2
                    pd, pdn = P[4 + dpar], "P%d" % (4 + dpar)
                    fdeps = [B.dep_r("fuse_t%d" % i)[0] for i in range(16)]
                    deps = B.dep_r("wdb%d" % dpar) + fdeps + B.dep_w(pdn)
                    def mmd(t, p_=pd, w_=wdb[dpar]):
                        r = None
                        for i in range(16):
                            r = t.matmul(p_[:], w_[:, i * 128:(i + 1) * 128],
                                         fuse[:, i * RK:(i + 1) * RK],
                                         start=(i == 0), stop=(i == 15))
                        return r
                    mk = B.emit("t", deps, mmd, "T")
                    B.did_w(mk, pdn)
                    B.did_r(mk, "wdb%d" % dpar, *["fuse_t%d" % i for i in range(16)])
                    nidx = grp * 16 + ot + 2
                    if nidx < 64:
                        dma_wd_piece(nidx, nidx // 16, (nidx % 16) and None or None) if False else None
                    if ot + 2 < NT:
                        dma_wd_piece(ot + 2, grp, dpar)
                    elif grp + 1 < 4:
                        dma_wd_piece(ot + 2 - NT, grp + 1, dpar)
                    if grp == 0:
                        mk = B.emit("s", B.dep_r(pdn) + B.dep_w("acc_t%d" % ot),
                                    lambda s, o_=ot, p_=pd: s.copy(acc[:, o_ * RK:(o_ + 1) * RK], p_[:]), "S")
                    else:
                        mk = B.emit("v", B.dep_r(pdn, "acc_t%d" % ot),
                                    lambda v, o_=ot, p_=pd: v.tensor_tensor(
                                        acc[:, o_ * RK:(o_ + 1) * RK], acc[:, o_ * RK:(o_ + 1) * RK],
                                        p_[:], op=ALU.add), "V")
                    B.did_r(mk, pdn); B.did_w(mk, "acc_t%d" % ot)
            for ot in range(NT):
                par = ot % 2
                mk = B.emit("q", B.dep_w("ard%d" % par) + B.dep_r("ascr_t%d" % ot),
                            lambda q, o_=ot, a_=a_rd[par]: q.dma_start(
                                out=a_[:], in_=ascr[o_ * 128:(o_ + 1) * 128, :]), "DB%d" % par)
                B.did_w(mk, "ard%d" % par); B.did_r(mk, "ascr_t%d" % ot)
                mk = B.emit("g", B.dep_w("xrd%d" % par),
                            lambda g, o_=ot, x_=x_rd[par]: g.dma_start(
                                out=x_[:], in_=xkv[o_ * 128:(o_ + 1) * 128, 3072 + outcol: 3072 + outcol + RK]),
                            "DB%d" % par)
                B.did_w(mk, "xrd%d" % par)
                mk = B.emit("v", B.dep_r("ard%d" % par, "xrd%d" % par) + B.dep_w("axr%d" % par),
                            lambda v, a_=a_rd[par], x_=x_rd[par], r_=axr[par]: v.tensor_tensor(
                                r_[:], a_[:], x_[:], op=ALU.add), "V")
                B.did_r(mk, "ard%d" % par, "xrd%d" % par); B.did_w(mk, "axr%d" % par)
                mk = B.emit("v", B.dep_r("acc_t%d" % ot, "sdv", "axr%d" % par) + B.dep_w("ost%d" % par),
                            lambda v, o_=ot, r_=axr[par], u_=outst[par]: v.scalar_tensor_tensor(
                                u_[:], acc[:, o_ * RK:(o_ + 1) * RK], sdv[:, o_:o_ + 1], r_[:],
                                op0=ALU.mult, op1=ALU.add), "V")
                B.did_r(mk, "acc_t%d" % ot, "sdv", "axr%d" % par); B.did_w(mk, "ost%d" % par)
                mk = B.emit("q", B.dep_r("ost%d" % par),
                            lambda q, o_=ot, u_=outst[par]: q.dma_start(
                                out=out[o_ * 128:(o_ + 1) * 128, outcol:outcol + RK], in_=u_[:]), "DO")
                B.did_r(mk, "ost%d" % par)

        dma_x_chunk(0, *xslot(0))
        dma_x_chunk(1, *xslot(1))
        k_front(0)
        for c in range(6):
            if c + 1 <= 6:
                k_front(c + 1)
            k_tail(c)
        k_tail(6)
        q_chunk(0)
        f_chunk(0)
        dma_x_chunk(7, *xslot(7))
        k_front(7)
        k_tail(7)
        q_chunk(1)
        f_chunk(1)
        B.emit("q", [("DO", B.cnt["DO"])], lambda q: q.nop(), "G")
        B.flush()
    return nc


_NC_CACHE = None


def kernel(L0_attn_x, L0_attn_W_q, L0_attn_W_kv, L0_attn_W_o,
           L0_ffn_w_gate, L0_ffn_w_up, L0_ffn_w_down,
           L0_ffn_s_gate, L0_ffn_s_up, L0_ffn_s_down):
    x = np.asarray(L0_attn_x, dtype=np.float32)
    Wq = np.asarray(L0_attn_W_q, dtype=np.float32)
    Wkv = np.asarray(L0_attn_W_kv, dtype=np.float32)
    Wo = np.asarray(L0_attn_W_o, dtype=np.float32)
    wg = np.asarray(L0_ffn_w_gate, dtype=np.int8)
    wu = np.asarray(L0_ffn_w_up, dtype=np.int8)
    wd = np.asarray(L0_ffn_w_down, dtype=np.int8)
    sg = np.asarray(L0_ffn_s_gate, dtype=np.float32)
    su = np.asarray(L0_ffn_s_up, dtype=np.float32)
    sdn = np.asarray(L0_ffn_s_down, dtype=np.float32)

    t4a = lambda W: np.ascontiguousarray(W.reshape(16, 128, 16, 128).transpose(0, 3, 2, 1))
    wq_t, wkv_t, wo_t = t4a(Wq), t4a(Wkv), t4a(Wo)
    wg_t = np.ascontiguousarray(wg.reshape(16, 128, 64, 128).transpose(2, 1, 0, 3))
    wu_t = np.ascontiguousarray(wu.reshape(16, 128, 64, 128).transpose(2, 1, 0, 3))
    wd_t = np.ascontiguousarray(wd.reshape(64, 128, 16, 128).transpose(2, 1, 0, 3))
    sg_t = np.ascontiguousarray(sg.reshape(64, 128).T)
    su_t = np.ascontiguousarray(su.reshape(64, 128).T)
    sd_t = np.ascontiguousarray(sdn.reshape(16, 128).T)
    n = np.arange(128)
    ang = 2.0 * np.pi * np.outer(n, n) / 128.0
    cf = np.cos(ang).astype(np.float32)
    sf = np.sin(ang).astype(np.float32)

    in_maps = []
    for c in range(8):
        b, j = c // 4, c % 4
        xb = x[b]
        rolled = np.concatenate([xb[0:1024]] * (3 - j) + [xb[0:(j + 1) * 1024]], axis=0)
        x_kv = np.ascontiguousarray(rolled.T)
        m = np.zeros((16, SEQ), dtype=np.float32)
        m[:, (3 - j) * 1024:] = 1.0
        in_maps.append({
            "xkv": x_kv, "wq": wq_t, "wkv": wkv_t, "wo": wo_t,
            "wg": wg_t, "wu": wu_t, "wd": wd_t,
            "sg": sg_t, "su": su_t, "sd": sd_t,
            "cf": cf, "sf": sf, "mask": m,
        })

    global _NC_CACHE
    if _NC_CACHE is None:
        _NC_CACHE = build_program()
    res = run_bass_kernel_spmd(_NC_CACHE, in_maps, core_ids=list(range(8)))
    outp = np.empty((2, SEQ, D), dtype=np.float32)
    for c in range(8):
        b, j = c // 4, c % 4
        outp[b, j * 1024:(j + 1) * 1024, :] = np.asarray(res.results[c]["out"]).T
    return outp


# version 08
# speedup vs baseline: 1.0434x; 1.0434x over previous
import contextlib
import numpy as np
import concourse.bass as bass
import concourse.mybir as mybir
from concourse.bass_utils import run_bass_kernel_spmd

F32 = mybir.dt.float32
BF16 = mybir.dt.bfloat16
I8 = mybir.dt.int8
AF = mybir.ActivationFunctionType
ALU = mybir.AluOpType

D = 2048
NT = 16
SEQ = 4096
RK = 512
NCH = SEQ // RK
RF = 256
C_Q = float(128 ** 3)
C_KV = 128.0

_EK = ("t", "v", "s", "g", "q")


class Bld:
    def __init__(self, nc, block, semnames, es):
        self.nc = nc
        self.sec = {"t": block.tensor, "v": block.vector, "s": block.scalar,
                    "g": block.gpsimd, "q": block.sync}
        self.sem, self.cnt = {}, {}
        self.seen = {k: {} for k in _EK}
        for n in semnames:
            self.sem[n] = es.enter_context(nc.semaphore(n))
            self.cnt[n] = 0
        self.wr, self.rd = {}, {}
        self.pend = {}

    def emit(self, ekey, deps, fn, inc):
        step = 16 if inc.startswith("D") else 1
        todo, seen = [], self.seen[ekey]
        for d in deps:
            if d is None:
                continue
            s, v = d
            if v > 0 and seen.get(s, 0) < v:
                todo.append((s, v))
                seen[s] = v
        semh, inc_sem = self.sem, self.sem[inc]

        def body(eng):
            for s, v in todo:
                eng.wait_ge(semh[s], v)
            fn(eng).then_inc(inc_sem, step)

        self.pend.setdefault(ekey, []).append(body)
        self.cnt[inc] += step
        return (inc, self.cnt[inc])

    def flush(self):
        for ekey, bodies in self.pend.items():
            def run_all(eng, bodies_=bodies):
                for b in bodies_:
                    b(eng)
            self.sec[ekey](run_all)
        self.pend = {}

    def dep_r(self, *bufs):
        return [self.wr.get(b) for b in bufs]

    def dep_w(self, *bufs):
        out = []
        for b in bufs:
            out.extend(self.rd.get(b, {}).items())
            if b in self.wr:
                out.append(self.wr[b])
        return out

    def did_r(self, mark, *bufs):
        for b in bufs:
            d = self.rd.setdefault(b, {})
            if d.get(mark[0], 0) < mark[1]:
                d[mark[0]] = mark[1]

    def did_w(self, mark, *bufs):
        for b in bufs:
            self.wr[b] = mark
            self.rd.pop(b, None)


def build_program():
    nc = bass.Bass("TRN2", target_bir_lowering=False)
    dti = lambda n, s, d: nc.dram_tensor(n, s, d, kind="ExternalInput").ap()
    xkv = dti("xkv", [D, SEQ], F32)
    wq = dti("wq", [16, 128, 16, 128], F32)
    wkv = dti("wkv", [16, 128, 16, 128], F32)
    wo = dti("wo", [16, 128, 16, 128], F32)
    wg = dti("wg", [64, 128, 16, 128], I8)
    wu = dti("wu", [64, 128, 16, 128], I8)
    wd = dti("wd", [16, 128, 64, 128], I8)
    sg = dti("sg", [128, 64], F32)
    su = dti("su", [128, 64], F32)
    sd = dti("sd", [128, 16], F32)
    cfm = dti("cf", [128, 128], F32)
    sfm = dti("sf", [128, 128], F32)
    mask = dti("mask", [16, SEQ], F32)
    out = nc.dram_tensor("out", [D, 1024], F32, kind="ExternalOutput").ap()
    nscr = nc.dram_tensor("nscr", [2, 16, RK], BF16, kind="Internal").ap()
    rnscr = nc.dram_tensor("rnscr", [2, 16, RK], BF16, kind="Internal").ap()
    ascr = nc.dram_tensor("ascr", [D, RK], BF16, kind="Internal").ap()

    sems = ["T", "V", "S", "G", "DC",
            "DX0", "DX1", "DW0", "DW1", "DG0", "DG1", "DU0", "DU1",
            "DD0", "DD1", "DN0", "DN1", "DR0", "DR1", "DA0", "DA1",
            "DB0", "DB1", "DM0", "DM1", "DO"]

    es = contextlib.ExitStack()
    with es:
        sb = lambda n, s, d: es.enter_context(nc.sbuf_tensor(n, s, d))
        xn_roll = sb("xn_roll", [128, NT * RK], BF16)
        xn_own = sb("xn_own", [128, NT * RK], BF16)
        kvb = sb("kvb", [128, NT * RK], BF16)
        carry = [sb("carry0", [128, NT], F32), sb("carry1", [128, NT], F32)]
        ctmp = sb("ctmp", [128, NT], F32)
        sqt = [sb("sqt0", [128, RK], BF16), sb("sqt1", [128, RK], BF16)]
        rnb = [sb("rnb0", [128, RK], BF16), sb("rnb1", [128, RK], BF16)]
        mb_sb = sb("mb_sb", [128, RK], BF16)
        msqv = sb("msqv", [128, RK], F32)
        rstd_b = sb("rstd_b", [128, RK], BF16)
        qsb = sb("qsb", [128, NT * RK], BF16)
        vh = sb("vh", [128, NT * RK], BF16)
        hb = [[sb(f"hb{p}_{i}", [128, RK], BF16) for i in range(6)] for p in range(2)]
        attn_st = [sb("ast0", [128, RK], BF16)]
        wbuf = [sb("wb0", [128, 2048], BF16), sb("wb1", [128, 2048], BF16)]
        wgb = [sb("wgb0", [128, 2048], BF16), sb("wgb1", [128, 2048], BF16)]
        wub = [sb("wub0", [128, 2048], BF16), sb("wub1", [128, 2048], BF16)]
        wdb = [sb("wdb0", [128, 2048], BF16), sb("wdb1", [128, 2048], BF16)]
        fuse = sb("fuse", [128, 16 * RK], BF16)
        silg = [sb("silg0", [128, RK], BF16), sb("silg1", [128, RK], BF16)]
        sgv = sb("sgv", [128, 64], F32)
        suv = sb("suv", [128, 64], F32)
        sdv = sb("sdv", [128, 16], F32)
        cfb = sb("cfb", [128, 128], BF16)
        sfb = sb("sfb", [128, 128], BF16)
        onesF = sb("onesF", [128, 128], BF16)
        onesS = sb("onesS", [128, RK], BF16)
        maskb = [sb("mkb0", [128, RK], BF16), sb("mkb1", [128, RK], BF16)]
        a_rd = [sb("ard0", [128, RK], BF16), sb("ard1", [128, RK], BF16)]
        x_rd = [sb("xrd0", [128, RK], BF16), sb("xrd1", [128, RK], BF16)]
        axr = [sb("axr0", [128, RK], F32), sb("axr1", [128, RK], F32)]
        outst = [sb("ost0", [128, RK], F32), sb("ost1", [128, RK], F32)]
        acc = sb("acc", [128, NT * RK], F32)
        P = [es.enter_context(nc.psum_tensor(f"P{i}", [128, 512], F32)) for i in range(8)]
        sbuf_left = nc.sbuf_bytes_remaining
        print('SBUF remaining after alloc:', sbuf_left)
        block = es.enter_context(nc.Block())
        B = Bld(nc, block, sems, es)
        B.sbuf_left = sbuf_left

        def prolog(g):
            g.memset(onesF[:], 1.0)
            g.memset(onesS[:], 1.0)
            g.memset(carry[0][:], 0.0)
            return g.memset(carry[1][:], 0.0)
        mk = B.emit("g", [], prolog, "G")
        B.did_w(mk, "onesF", "onesS", "carry0", "carry1")
        for dst, src, nm in ((sgv, sg, "sgv"), (suv, su, "suv"), (sdv, sd, "sdv"),
                             (cfb, cfm, "cfb"), (sfb, sfm, "sfb")):
            mk = B.emit("g", [], (lambda d_, s_: (lambda g: g.dma_start(out=d_[:], in_=s_)))(dst, src), "DC")
            B.did_w(mk, nm)

        def dma_x_chunk(c, dst, xname):
            sem = "DX%d" % (c % 2)
            mk = B.emit("g", B.dep_w(xname),
                        lambda g, d_=dst, c_=c: g.dma_start(
                            out=d_[:, 0:NT * RK].rearrange("p (t r) -> p t r", t=NT),
                            in_=xkv[:, c_ * RK:(c_ + 1) * RK].rearrange("(t p) r -> p t r", p=128)),
                        sem)
            B.did_w(mk, xname, *[xname + "_t%d" % i for i in range(NT)])
            return mk

        def dma_mask(c):
            p = c % 2
            mk = B.emit("g", B.dep_w("mkb%d" % p),
                        lambda g, c_=c, p_=p: g.dma_start(
                            out=maskb[p_][:],
                            in_=mask[0, c_ * RK:(c_ + 1) * RK].unsqueeze(0).broadcast_to((128, RK))),
                        "DM%d" % p)
            B.did_w(mk, "mkb%d" % p)

        def dma_wtile(wsrc, mt, bufs, bname, sem_base):
            par = mt % 2
            nm = bname + str(par)
            mk = B.emit("g", B.dep_w(nm),
                        lambda g, w_=wsrc, m_=mt, d_=bufs[par]: g.dma_start(
                            out=d_[:], in_=w_[m_].rearrange("p a b -> p (a b)")),
                        "%s%d" % (sem_base, par))
            B.did_w(mk, nm)

        def dma_wd_piece(ot, grp, par):
            nm = "wdb%d" % par
            mk = B.emit("g", B.dep_w(nm),
                        lambda g, o_=ot, p_=grp, d_=wdb[par]: g.dma_start(
                            out=d_[:], in_=wd[o_, :, p_ * 16:(p_ + 1) * 16, :].rearrange("p a b -> p (a b)")),
                        "DD%d" % par)
            B.did_w(mk, nm)

        def ln_chunk(c, xdst, xname, own):
            def mm_sum(t, x_=xdst):
                r = None
                for kt in range(NT):
                    r = t.matmul(P[2][:], onesF[:], x_[:, kt * RK:(kt + 1) * RK],
                                 start=(kt == 0), stop=(kt == NT - 1))
                return r
            mk = B.emit("t", B.dep_r(xname, "onesF") + B.dep_w("P2"), mm_sum, "T")
            B.did_w(mk, "P2"); B.did_r(mk, xname, "onesF")
            if own:
                mk = B.emit("s", B.dep_r("P2") + B.dep_w("mb_sb"),
                            lambda s: s.activation(mb_sb[:], P[2][:], AF.Copy, scale=1.0 / D), "S")
                B.did_w(mk, "mb_sb"); B.did_r(mk, "P2")
                for kt in range(NT):
                    p = kt % 2
                    mk = B.emit("v", B.dep_r(xname) + B.dep_w("sqt%d" % p),
                                lambda v, k_=kt, p_=p, x_=xdst: v.tensor_tensor(
                                    sqt[p_][:], x_[:, k_ * RK:(k_ + 1) * RK],
                                    x_[:, k_ * RK:(k_ + 1) * RK], op=ALU.mult), "V")
                    B.did_w(mk, "sqt%d" % p); B.did_r(mk, xname)
                    mk2 = B.emit("t", B.dep_r("sqt%d" % p, "onesF") + (B.dep_w("P3") if kt == 0 else []),
                                 lambda t, k_=kt, p_=p: t.matmul(P[3][:], onesF[:], sqt[p_][:],
                                                                 start=(k_ == 0), stop=(k_ == NT - 1)), "T")
                    B.did_r(mk2, "sqt%d" % p, "onesF")
                    if kt == NT - 1:
                        B.did_w(mk2, "P3")
                mk = B.emit("v", B.dep_r("mb_sb") + B.dep_w("msqv"),
                            lambda v: v.tensor_tensor(msqv[:], mb_sb[:], mb_sb[:], op=ALU.mult), "V")
                B.did_w(mk, "msqv"); B.did_r(mk, "mb_sb")
                mk = B.emit("v", B.dep_w("msqv"),
                            lambda v: v.tensor_scalar_add(msqv[:], msqv[:], -1e-5), "V")
                B.did_w(mk, "msqv")
                mk = B.emit("v", B.dep_r("msqv") + B.dep_w("P3"),
                            lambda v: v.scalar_tensor_tensor(P[3][:], P[3][:], 1.0 / D, msqv[:],
                                                             op0=ALU.mult, op1=ALU.subtract), "V")
                B.did_w(mk, "P3"); B.did_r(mk, "msqv")
                mk = B.emit("s", B.dep_w("P3"),
                            lambda s: s.activation(P[3][:], P[3][:], AF.Ln), "S")
                B.did_w(mk, "P3")
                mk = B.emit("s", B.dep_r("P3") + B.dep_w("rstd_b"),
                            lambda s: s.activation(rstd_b[:], P[3][:], AF.Exp, scale=-0.5), "S")
                B.did_w(mk, "rstd_b"); B.did_r(mk, "P3")
            for kt in range(NT):
                mk = B.emit("v", B.dep_w(xname) + B.dep_r("P2"),
                            lambda v, k_=kt, x_=xdst: v.scalar_tensor_tensor(
                                x_[:, k_ * RK:(k_ + 1) * RK], P[2][:], -1.0 / D,
                                x_[:, k_ * RK:(k_ + 1) * RK], op0=ALU.mult, op1=ALU.add), "V")
                B.did_w(mk, xname) if kt == NT - 1 else B.did_r(mk, xname)
                B.did_r(mk, "P2")
                if own:
                    mk = B.emit("v", B.dep_r("rstd_b"),
                                lambda v, k_=kt, x_=xdst: v.tensor_tensor(
                                    x_[:, k_ * RK:(k_ + 1) * RK], x_[:, k_ * RK:(k_ + 1) * RK],
                                    rstd_b[:], op=ALU.mult), "V")
                    B.did_w(mk, xname) if kt == NT - 1 else B.did_r(mk, xname)
                    B.did_r(mk, "rstd_b")
            for i in range(NT):
                B.did_w(B.wr[xname], xname + "_t%d" % i)

        def proj(xsrc, xname, wsrc, dst, dstname, psA, psB):
            dma_wtile(wsrc, 0, wbuf, "wb", "DW")
            dma_wtile(wsrc, 1, wbuf, "wb", "DW")
            pps = [psA, psB]
            for mt in range(NT):
                par = mt % 2
                pp = P[pps[par]]
                pn = "P%d" % pps[par]
                def mm(t, p_=pp, w_=wbuf[par], x_=xsrc):
                    r = None
                    for kt in range(NT):
                        r = t.matmul(p_[:], w_[:, kt * 128:(kt + 1) * 128],
                                     x_[:, kt * RK:(kt + 1) * RK],
                                     start=(kt == 0), stop=(kt == NT - 1))
                    return r
                mk = B.emit("t", B.dep_r("wb%d" % par, xname) + B.dep_w(pn), mm, "T")
                B.did_w(mk, pn); B.did_r(mk, "wb%d" % par, xname)
                if mt + 2 < NT:
                    dma_wtile(wsrc, mt + 2, wbuf, "wb", "DW")
                mk = B.emit("s", B.dep_r(pn) + B.dep_w(dstname + "_t%d" % mt),
                            lambda s, m_=mt, p_=pp, d_=dst: s.copy(d_[:, m_ * RK:(m_ + 1) * RK], p_[:]), "S")
                B.did_r(mk, pn); B.did_w(mk, dstname + "_t%d" % mt)

        def norms(srcbuf, srcname, cpar, sq_scale, maskname):
            for mt in range(NT):
                p = mt % 2
                mk = B.emit("v", B.dep_r(srcname + "_t%d" % mt) + B.dep_w("sqt%d" % p),
                            lambda v, m_=mt, p_=p, x_=srcbuf: v.tensor_tensor(
                                sqt[p_][:], x_[:, m_ * RK:(m_ + 1) * RK],
                                x_[:, m_ * RK:(m_ + 1) * RK], op=ALU.mult), "V")
                B.did_w(mk, "sqt%d" % p); B.did_r(mk, srcname + "_t%d" % mt)
                pn = "P%d" % (6 + p)
                mk = B.emit("t", B.dep_r("sqt%d" % p, "onesF") + B.dep_w(pn),
                            lambda t, p_=p: t.matmul(P[6 + p_][:], onesF[:], sqt[p_][:],
                                                     start=True, stop=True), "T")
                B.did_w(mk, pn); B.did_r(mk, "sqt%d" % p, "onesF")
                mk = B.emit("s", B.dep_w(pn),
                            lambda s, p_=p, sc_=float(sq_scale): s.activation(
                                P[6 + p_][:], P[6 + p_][:], AF.Ln, scale=sc_), "S")
                B.did_w(mk, pn)
                mk = B.emit("s", B.dep_r(pn) + B.dep_w("rnb%d" % p),
                            lambda s, p_=p: s.activation(rnb[p_][:], P[6 + p_][:], AF.Exp, scale=-0.5), "S")
                B.did_w(mk, "rnb%d" % p); B.did_r(mk, pn)
                mk = B.emit("v", B.dep_r("rnb%d" % p) + B.dep_w(srcname + "_t%d" % mt),
                            lambda v, m_=mt, p_=p, x_=srcbuf: v.tensor_tensor(
                                x_[:, m_ * RK:(m_ + 1) * RK], x_[:, m_ * RK:(m_ + 1) * RK],
                                rnb[p_][:], op=ALU.mult), "V")
                B.did_w(mk, srcname + "_t%d" % mt); B.did_r(mk, "rnb%d" % p)
                if maskname is not None:
                    mk = B.emit("v", B.dep_r(maskname) + B.dep_w(srcname + "_t%d" % mt),
                                lambda v, m_=mt, mi_=int(maskname[-1]), x_=srcbuf: v.tensor_tensor(
                                    x_[:, m_ * RK:(m_ + 1) * RK], x_[:, m_ * RK:(m_ + 1) * RK],
                                    maskb[mi_][:], op=ALU.mult), "V")
                    B.did_w(mk, srcname + "_t%d" % mt); B.did_r(mk, maskname)

        def xslot(c):
            if c >= 6:
                return xn_own, "xn_own"
            return (xn_roll, "xn_roll") if c % 2 == 0 else (qsb, "qsb")

        def kvslot(c):
            if c >= 6 or c % 2 == 0:
                return kvb, "kvb"
            return vh, "vh"

        def k_front(c):
            own = c >= 6
            xdst, xname = xslot(c)
            kdst, kn = kvslot(c)
            dma_mask(c)
            ln_chunk(c, xdst, xname, own)
            proj(xdst, xname, wkv, kdst, kn, 0, 1)
            if c + 2 <= 6:
                dma_x_chunk(c + 2, *xslot(c + 2))

        def k_tail(c):
            cpar = c % 2
            kdst, kn = kvslot(c)
            norms(kdst, kn, cpar, C_KV, "mkb%d" % cpar)
            cold, cnew = carry[c % 2], carry[1 - c % 2]
            coldn, cnewn = "carry%d" % (c % 2), "carry%d" % (1 - c % 2)
            for mt in range(NT):
                p = mt % 2
                mk = B.emit("s", B.dep_r(kn + "_t%d" % mt) + B.dep_w("ctmp", "sqt%d" % p),
                            lambda s, m_=mt, p_=p, x_=kdst: s.activation(
                                sqt[p_][:], x_[:, m_ * RK:(m_ + 1) * RK], AF.Copy,
                                accum_out=ctmp[:, m_:m_ + 1]), "S")
                B.did_w(mk, "ctmp", "sqt%d" % p); B.did_r(mk, kn + "_t%d" % mt)
                mk = B.emit("g", B.dep_r("ctmp", coldn) + (B.dep_w(cnewn) if mt == 0 else []),
                            lambda g, m_=mt, co_=cold, cn_=cnew: g.tensor_tensor(
                                cn_[:, m_:m_ + 1], co_[:, m_:m_ + 1], ctmp[:, m_:m_ + 1], op=ALU.add), "G")
                B.did_r(mk, "ctmp", coldn)
                if mt == NT - 1:
                    B.did_w(mk, cnewn)

        def own_scans(c):
            cold = carry[c % 2]
            coldn = "carry%d" % (c % 2)
            for mt in range(NT):
                mk = B.emit("v", B.dep_w("kvb_t%d" % mt) + B.dep_r(coldn, "onesS"),
                            lambda v, m_=mt, co_=cold: v.tensor_tensor_scan(
                                kvb[:, m_ * RK:(m_ + 1) * RK], onesS[:],
                                kvb[:, m_ * RK:(m_ + 1) * RK], co_[:, m_:m_ + 1],
                                op0=ALU.mult, op1=ALU.add), "V")
                B.did_w(mk, "kvb_t%d" % mt)
                B.did_r(mk, coldn, "onesS")

        def q_chunk(oc):
            cpar = oc
            kname = "kvb"
            proj(xn_own, "xn_own", wq, qsb, "qsb", 0, 1)
            norms(qsb, "qsb", cpar, C_Q, None)
            own_scans(6 + oc)
            for mt in range(NT):
                p = mt % 2
                names = [f"hb{p}_{i}" for i in range(6)]
                pa, pb = P[2 * p], P[2 * p + 1]
                pan, pbn = "P%d" % (2 * p), "P%d" % (2 * p + 1)
                for (src, srcn, mat, matn, dsti) in (
                        (qsb, "qsb_t%d" % mt, cfb, "cfb", 0),
                        (qsb, "qsb_t%d" % mt, sfb, "sfb", 1),
                        (kvb, kname + "_t%d" % mt, cfb, "cfb", 2),
                        (kvb, kname + "_t%d" % mt, sfb, "sfb", 3)):
                    pp, pn = (pa, pan) if dsti % 2 == 0 else (pb, pbn)
                    mk = B.emit("t", B.dep_r(srcn, matn) + B.dep_w(pn),
                                lambda t, s_=src, m_=mt, w_=mat, p_=pp: t.matmul(
                                    p_[:], w_[:], s_[:, m_ * RK:(m_ + 1) * RK], start=True, stop=True), "T")
                    B.did_w(mk, pn); B.did_r(mk, srcn, matn)
                    mk = B.emit("s", B.dep_r(pn) + B.dep_w(names[dsti]),
                                lambda s, b_=hb[p][dsti], p_=pp: s.copy(b_[:], p_[:]), "S")
                    B.did_r(mk, pn); B.did_w(mk, names[dsti])
                def prods(v, b_=hb[p]):
                    QC, QS, KC, KS, T5, T6 = b_
                    v.tensor_tensor(T5[:], QC[:], KC[:], op=ALU.mult)
                    v.tensor_tensor(T6[:], QS[:], KS[:], op=ALU.mult)
                    v.tensor_tensor(T5[:], T5[:], T6[:], op=ALU.subtract)
                    v.tensor_tensor(T6[:], QC[:], KS[:], op=ALU.mult)
                    v.tensor_tensor(QC[:], QS[:], KC[:], op=ALU.mult)
                    return v.tensor_tensor(T6[:], T6[:], QC[:], op=ALU.add)
                mk = B.emit("v", B.dep_r(*names) + B.dep_w(names[0], names[4], names[5]), prods, "V")
                B.did_r(mk, *names); B.did_w(mk, names[0], names[4], names[5])
                mk = B.emit("t", B.dep_r(names[4], names[5], "cfb", "sfb") + B.dep_w("P4"),
                            lambda t, b_=hb[p]: (t.matmul(P[4][:], cfb[:], b_[4][:], start=True, stop=False),
                                                 t.matmul(P[4][:], sfb[:], b_[5][:], start=False, stop=True))[-1], "T")
                B.did_w(mk, "P4"); B.did_r(mk, names[4], names[5], "cfb", "sfb")
                mk = B.emit("s", B.dep_r("P4") + B.dep_w("vh_t%d" % mt),
                            lambda s, m_=mt: s.copy(vh[:, m_ * RK:(m_ + 1) * RK], P[4][:]), "S")
                B.did_r(mk, "P4"); B.did_w(mk, "vh_t%d" % mt)
            vh_names = ["vh_t%d" % k for k in range(NT)]
            dma_wtile(wo, 0, wbuf, "wb", "DW")
            dma_wtile(wo, 1, wbuf, "wb", "DW")
            for mt in range(NT):
                par = mt % 2
                pp, pn = P[6 + par], "P%d" % (6 + par)
                def mmo(t, p_=pp, w_=wbuf[par]):
                    r = None
                    for kt in range(NT):
                        r = t.matmul(p_[:], w_[:, kt * 128:(kt + 1) * 128],
                                     vh[:, kt * RK:(kt + 1) * RK], start=(kt == 0), stop=(kt == NT - 1))
                    return r
                mk = B.emit("t", B.dep_r("wb%d" % par, *vh_names) + B.dep_w(pn), mmo, "T")
                B.did_w(mk, pn); B.did_r(mk, "wb%d" % par, *vh_names)
                if mt + 2 < NT:
                    dma_wtile(wo, mt + 2, wbuf, "wb", "DW")
                mk = B.emit("s", B.dep_r(pn) + B.dep_w("ast0"),
                            lambda s, p_=pp, a_=attn_st[0]: s.copy(a_[:], p_[:]), "S")
                B.did_r(mk, pn); B.did_w(mk, "ast0")
                mk = B.emit("q", B.dep_r("ast0") + B.dep_w("ascr_t%d" % mt),
                            lambda q, m_=mt, a_=attn_st[0]: q.dma_start(
                                out=ascr[m_ * 128:(m_ + 1) * 128, :], in_=a_[:]), "DA%d" % par)
                B.did_r(mk, "ast0"); B.did_w(mk, "ascr_t%d" % mt)

        def f_chunk(oc):
            outcol = oc * RK
            dma_wtile(wg, 0, wgb, "wgb", "DG")
            dma_wtile(wu, 0, wub, "wub", "DU")
            dma_wtile(wg, 1, wgb, "wgb", "DG")
            dma_wtile(wu, 1, wub, "wub", "DU")
            for grp in range(4):
                for i in range(16):
                    mt = grp * 16 + i
                    par = mt % 2
                    pg, pgn = P[par], "P%d" % par
                    pu, pun = P[2 + par], "P%d" % (2 + par)
                    def mmg(t, p_=pg, w_=wgb[par]):
                        r = None
                        for kt in range(NT):
                            r = t.matmul(p_[:], w_[:, kt * 128:(kt + 1) * 128],
                                         xn_own[:, kt * RK:(kt + 1) * RK],
                                         start=(kt == 0), stop=(kt == NT - 1))
                        return r
                    mk = B.emit("t", B.dep_r("wgb%d" % par, "xn_own") + B.dep_w(pgn), mmg, "T")
                    B.did_w(mk, pgn); B.did_r(mk, "wgb%d" % par, "xn_own")
                    if mt + 2 < 64:
                        dma_wtile(wg, mt + 2, wgb, "wgb", "DG")
                    def mmu(t, p_=pu, w_=wub[par]):
                        r = None
                        for kt in range(NT):
                            r = t.matmul(p_[:], w_[:, kt * 128:(kt + 1) * 128],
                                         xn_own[:, kt * RK:(kt + 1) * RK],
                                         start=(kt == 0), stop=(kt == NT - 1))
                        return r
                    mk = B.emit("t", B.dep_r("wub%d" % par, "xn_own") + B.dep_w(pun), mmu, "T")
                    B.did_w(mk, pun); B.did_r(mk, "wub%d" % par, "xn_own")
                    if mt + 2 < 64:
                        dma_wtile(wu, mt + 2, wub, "wub", "DU")
                    mk = B.emit("s", B.dep_r(pgn, "sgv") + B.dep_w("silg%d" % par),
                                lambda s, m_=mt, p_=pg, g_=silg[par]: s.activation(
                                    g_[:], p_[:], AF.Silu, scale=sgv[:, m_:m_ + 1]), "S")
                    B.did_r(mk, pgn, "sgv"); B.did_w(mk, "silg%d" % par)
                    mk = B.emit("v", B.dep_r(pun, "suv", "silg%d" % par) + B.dep_w("fuse_t%d" % i),
                                lambda v, m_=mt, i_=i, p_=pu, g_=silg[par]: v.scalar_tensor_tensor(
                                    fuse[:, i_ * RK:(i_ + 1) * RK], p_[:], suv[:, m_:m_ + 1], g_[:],
                                    op0=ALU.mult, op1=ALU.mult), "V")
                    B.did_r(mk, pun, "suv", "silg%d" % par); B.did_w(mk, "fuse_t%d" % i)
                if grp == 0:
                    dma_wd_piece(0, grp, 0)
                    dma_wd_piece(1, grp, 1)
                for ot in range(NT):
                    dpar = ot % 2
                    pd, pdn = P[4 + dpar], "P%d" % (4 + dpar)
                    fdeps = [B.dep_r("fuse_t%d" % i)[0] for i in range(16)]
                    deps = B.dep_r("wdb%d" % dpar) + fdeps + B.dep_w(pdn)
                    def mmd(t, p_=pd, w_=wdb[dpar]):
                        r = None
                        for i in range(16):
                            r = t.matmul(p_[:], w_[:, i * 128:(i + 1) * 128],
                                         fuse[:, i * RK:(i + 1) * RK],
                                         start=(i == 0), stop=(i == 15))
                        return r
                    mk = B.emit("t", deps, mmd, "T")
                    B.did_w(mk, pdn)
                    B.did_r(mk, "wdb%d" % dpar, *["fuse_t%d" % i for i in range(16)])
                    nidx = grp * 16 + ot + 2
                    if nidx < 64:
                        dma_wd_piece(nidx, nidx // 16, (nidx % 16) and None or None) if False else None
                    if ot + 2 < NT:
                        dma_wd_piece(ot + 2, grp, dpar)
                    elif grp + 1 < 4:
                        dma_wd_piece(ot + 2 - NT, grp + 1, dpar)
                    if grp == 0:
                        mk = B.emit("s", B.dep_r(pdn) + B.dep_w("acc_t%d" % ot),
                                    lambda s, o_=ot, p_=pd: s.copy(acc[:, o_ * RK:(o_ + 1) * RK], p_[:]), "S")
                    else:
                        mk = B.emit("v", B.dep_r(pdn, "acc_t%d" % ot),
                                    lambda v, o_=ot, p_=pd: v.tensor_tensor(
                                        acc[:, o_ * RK:(o_ + 1) * RK], acc[:, o_ * RK:(o_ + 1) * RK],
                                        p_[:], op=ALU.add), "V")
                    B.did_r(mk, pdn); B.did_w(mk, "acc_t%d" % ot)
            for ot in range(NT):
                par = ot % 2
                mk = B.emit("q", B.dep_w("ard%d" % par) + B.dep_r("ascr_t%d" % ot),
                            lambda q, o_=ot, a_=a_rd[par]: q.dma_start(
                                out=a_[:], in_=ascr[o_ * 128:(o_ + 1) * 128, :]), "DB%d" % par)
                B.did_w(mk, "ard%d" % par); B.did_r(mk, "ascr_t%d" % ot)
                mk = B.emit("g", B.dep_w("xrd%d" % par),
                            lambda g, o_=ot, x_=x_rd[par]: g.dma_start(
                                out=x_[:], in_=xkv[o_ * 128:(o_ + 1) * 128, 3072 + outcol: 3072 + outcol + RK]),
                            "DB%d" % par)
                B.did_w(mk, "xrd%d" % par)
                mk = B.emit("v", B.dep_r("ard%d" % par, "xrd%d" % par) + B.dep_w("axr%d" % par),
                            lambda v, a_=a_rd[par], x_=x_rd[par], r_=axr[par]: v.tensor_tensor(
                                r_[:], a_[:], x_[:], op=ALU.add), "V")
                B.did_r(mk, "ard%d" % par, "xrd%d" % par); B.did_w(mk, "axr%d" % par)
                mk = B.emit("v", B.dep_r("acc_t%d" % ot, "sdv", "axr%d" % par) + B.dep_w("ost%d" % par),
                            lambda v, o_=ot, r_=axr[par], u_=outst[par]: v.scalar_tensor_tensor(
                                u_[:], acc[:, o_ * RK:(o_ + 1) * RK], sdv[:, o_:o_ + 1], r_[:],
                                op0=ALU.mult, op1=ALU.add), "V")
                B.did_r(mk, "acc_t%d" % ot, "sdv", "axr%d" % par); B.did_w(mk, "ost%d" % par)
                mk = B.emit("q", B.dep_r("ost%d" % par),
                            lambda q, o_=ot, u_=outst[par]: q.dma_start(
                                out=out[o_ * 128:(o_ + 1) * 128, outcol:outcol + RK], in_=u_[:]), "DO")
                B.did_r(mk, "ost%d" % par)

        dma_x_chunk(0, *xslot(0))
        dma_x_chunk(1, *xslot(1))
        k_front(0)
        for c in range(6):
            if c + 1 <= 6:
                k_front(c + 1)
            k_tail(c)
        k_tail(6)
        q_chunk(0)
        f_chunk(0)
        dma_x_chunk(7, *xslot(7))
        k_front(7)
        k_tail(7)
        q_chunk(1)
        f_chunk(1)
        B.emit("q", [("DO", B.cnt["DO"])], lambda q: q.nop(), "G")
        B.flush()
    return nc


_NC_CACHE = None


def kernel(L0_attn_x, L0_attn_W_q, L0_attn_W_kv, L0_attn_W_o,
           L0_ffn_w_gate, L0_ffn_w_up, L0_ffn_w_down,
           L0_ffn_s_gate, L0_ffn_s_up, L0_ffn_s_down):
    x = np.asarray(L0_attn_x, dtype=np.float32)
    Wq = np.asarray(L0_attn_W_q, dtype=np.float32)
    Wkv = np.asarray(L0_attn_W_kv, dtype=np.float32)
    Wo = np.asarray(L0_attn_W_o, dtype=np.float32)
    wg = np.asarray(L0_ffn_w_gate, dtype=np.int8)
    wu = np.asarray(L0_ffn_w_up, dtype=np.int8)
    wd = np.asarray(L0_ffn_w_down, dtype=np.int8)
    sg = np.asarray(L0_ffn_s_gate, dtype=np.float32)
    su = np.asarray(L0_ffn_s_up, dtype=np.float32)
    sdn = np.asarray(L0_ffn_s_down, dtype=np.float32)

    t4a = lambda W: np.ascontiguousarray(W.reshape(16, 128, 16, 128).transpose(0, 3, 2, 1))
    wq_t, wkv_t, wo_t = t4a(Wq), t4a(Wkv), t4a(Wo)
    wg_t = np.ascontiguousarray(wg.reshape(16, 128, 64, 128).transpose(2, 1, 0, 3))
    wu_t = np.ascontiguousarray(wu.reshape(16, 128, 64, 128).transpose(2, 1, 0, 3))
    wd_t = np.ascontiguousarray(wd.reshape(64, 128, 16, 128).transpose(2, 1, 0, 3))
    sg_t = np.ascontiguousarray(sg.reshape(64, 128).T)
    su_t = np.ascontiguousarray(su.reshape(64, 128).T)
    sd_t = np.ascontiguousarray(sdn.reshape(16, 128).T)
    n = np.arange(128)
    ang = 2.0 * np.pi * np.outer(n, n) / 128.0
    cf = np.cos(ang).astype(np.float32)
    sf = np.sin(ang).astype(np.float32)

    in_maps = []
    for c in range(8):
        b, j = c // 4, c % 4
        xb = x[b]
        rolled = np.concatenate([xb[0:1024]] * (3 - j) + [xb[0:(j + 1) * 1024]], axis=0)
        x_kv = np.ascontiguousarray(rolled.T)
        m = np.zeros((16, SEQ), dtype=np.float32)
        m[:, (3 - j) * 1024:] = 1.0
        in_maps.append({
            "xkv": x_kv, "wq": wq_t, "wkv": wkv_t, "wo": wo_t,
            "wg": wg_t, "wu": wu_t, "wd": wd_t,
            "sg": sg_t, "su": su_t, "sd": sd_t,
            "cf": cf, "sf": sf, "mask": m,
        })

    global _NC_CACHE
    if _NC_CACHE is None:
        _NC_CACHE = build_program()
    res = run_bass_kernel_spmd(_NC_CACHE, in_maps, core_ids=list(range(8)))
    outp = np.empty((2, SEQ, D), dtype=np.float32)
    for c in range(8):
        b, j = c // 4, c % 4
        outp[b, j * 1024:(j + 1) * 1024, :] = np.asarray(res.results[c]["out"]).T
    return outp


# version 09
# speedup vs baseline: 1.3889x; 1.3311x over previous
import contextlib
import numpy as np
import concourse.bass as bass
import concourse.mybir as mybir
from concourse.bass_utils import run_bass_kernel_spmd

F32 = mybir.dt.float32
BF16 = mybir.dt.bfloat16
I8 = mybir.dt.int8
AF = mybir.ActivationFunctionType
ALU = mybir.AluOpType

D = 2048
NT = 16
RK = 512
ROWS = 1024
C_Q = float(128 ** 3)
C_KV = 128.0
EPS = 1e-5

_EK = ("t", "v", "s", "g", "q")


class Bld:
    def __init__(self, nc, block, semnames, es):
        self.nc = nc
        self.sec = {"t": block.tensor, "v": block.vector, "s": block.scalar,
                    "g": block.gpsimd, "q": block.sync}
        self.sem, self.cnt = {}, {}
        self.seen = {k: {} for k in _EK}
        for n in semnames:
            self.sem[n] = es.enter_context(nc.semaphore(n))
            self.cnt[n] = 0
        self.wr, self.rd = {}, {}
        self.pend = {}

    def emit(self, ekey, deps, fn, inc):
        step = 16 if inc.startswith("D") else 1
        todo, seen = [], self.seen[ekey]
        for d in deps:
            if d is None:
                continue
            s, v = d
            if v > 0 and seen.get(s, 0) < v:
                todo.append((s, v))
                seen[s] = v
        semh, inc_sem = self.sem, self.sem[inc]

        def body(eng):
            for s, v in todo:
                eng.wait_ge(semh[s], v)
            fn(eng).then_inc(inc_sem, step)

        self.pend.setdefault(ekey, []).append(body)
        self.cnt[inc] += step
        return (inc, self.cnt[inc])

    def flush(self):
        for ekey, bodies in self.pend.items():
            def run_all(eng, bodies_=bodies):
                for b in bodies_:
                    b(eng)
            self.sec[ekey](run_all)
        self.pend = {}

    def dep_r(self, *bufs):
        return [self.wr.get(b) for b in bufs]

    def dep_w(self, *bufs):
        out = []
        for b in bufs:
            out.extend(self.rd.get(b, {}).items())
            if b in self.wr:
                out.append(self.wr[b])
        return out

    def did_r(self, mark, *bufs):
        for b in bufs:
            d = self.rd.setdefault(b, {})
            if d.get(mark[0], 0) < mark[1]:
                d[mark[0]] = mark[1]

    def did_w(self, mark, *bufs):
        for b in bufs:
            self.wr[b] = mark
            self.rd.pop(b, None)


def build_program():
    nc = bass.Bass("TRN2", target_bir_lowering=False)
    dti = lambda n, s, d: nc.dram_tensor(n, s, d, kind="ExternalInput").ap()
    x = dti("x", [D, ROWS], F32)
    wq = dti("wq", [16, 128, 16, 128], F32)
    wkv = dti("wkv", [16, 128, 16, 128], F32)
    wo = dti("wo", [16, 128, 16, 128], F32)
    wg = dti("wg", [64, 128, 16, 128], I8)
    wu = dti("wu", [64, 128, 16, 128], I8)
    wd = dti("wd", [16, 128, 64, 128], I8)
    sg = dti("sg", [128, 64], F32)
    su = dti("su", [128, 64], F32)
    sd = dti("sd", [128, 16], F32)
    cfm = dti("cf", [128, 128], F32)
    sfm = dti("sf", [128, 128], F32)
    mc = dti("mc", [128, 8], F32)
    out = nc.dram_tensor("out", [D, ROWS], F32, kind="ExternalOutput").ap()
    ascr = nc.dram_tensor("ascr", [D, ROWS], BF16, kind="Internal").ap()
    cc_in = nc.dram_tensor("cc_in", [128, 16], F32, kind="Internal")
    cc_out = nc.dram_tensor("cc_out", [8 * 128, 16], F32, kind="Internal")

    sems = ["T", "V", "S", "G", "CC", "DC",
            "DX0", "DX1", "DW0", "DW1", "DG0", "DG1", "DU0", "DU1",
            "DD0", "DD1", "DA0", "DA1", "DB0", "DB1",
            "DI", "DH", "DO"]

    es = contextlib.ExitStack()
    with es:
        sb = lambda n, s, d: es.enter_context(nc.sbuf_tensor(n, s, d))
        xnb = [sb("xn0", [128, NT * RK], BF16), sb("xn1", [128, NT * RK], BF16)]
        kvb = [sb("kv0", [128, NT * RK], BF16), sb("kv1", [128, NT * RK], BF16)]
        qr = sb("qr", [128, 4 * RK], BF16)
        sqr = sb("sqr", [128, 2 * RK], BF16)
        rcp = sb("rcp", [128, 2 * RK], F32)
        mbuf = [sb("mb0", [128, RK], F32), sb("mb1", [128, RK], F32)]
        vvb = [sb("vv0", [128, RK], F32), sb("vv1", [128, RK], F32)]
        mtmp = sb("mtmp", [128, RK], F32)
        rsdb = [sb("rsd0", [128, RK], F32), sb("rsd1", [128, RK], F32)]
        hb = [[sb(f"hb{p}_{i}", [128, RK], BF16) for i in range(6)] for p in range(2)]
        wbuf = [sb("wb%d" % i, [128, 2048], BF16) for i in range(4)]
        wgb = [sb("wgb0", [128, 2048], BF16), sb("wgb1", [128, 2048], BF16)]
        wub = [sb("wub0", [128, 2048], BF16), sb("wub1", [128, 2048], BF16)]
        fuse = sb("fuse", [128, 16 * RK], BF16)
        silg = [sb("silg0", [128, RK], BF16), sb("silg1", [128, RK], BF16)]
        acc = sb("acc", [128, NT * RK], F32)
        ast = [sb("ast0", [128, RK], BF16), sb("ast1", [128, RK], BF16)]
        a_rd = [sb("ard0", [128, RK], BF16), sb("ard1", [128, RK], BF16)]
        x_rd = [sb("xrd0", [128, RK], F32), sb("xrd1", [128, RK], F32)]
        axr = [sb("axr0", [128, RK], F32), sb("axr1", [128, RK], F32)]
        outst = [sb("ost0", [128, RK], F32), sb("ost1", [128, RK], F32)]
        tots = [sb("tot0", [128, 16], F32), sb("tot1", [128, 16], F32)]
        ctot = sb("ctot", [128, 16], F32)
        carry = [sb("carA", [128, 16], F32), sb("car1", [128, 16], F32)]
        gth = sb("gth", [128, 128], F32)
        sgv = sb("sgv", [128, 64], F32)
        suv = sb("suv", [128, 64], F32)
        sdv = sb("sdv", [128, 16], F32)
        cfb = sb("cfb", [128, 128], BF16)
        sfb = sb("sfb", [128, 128], BF16)
        mcb = sb("mcb", [128, 8], F32)
        onesF = sb("onesF", [128, 128], BF16)
        onesS = sb("onesS", [128, RK], BF16)
        P = [es.enter_context(nc.psum_tensor(f"P{i}", [128, 512], F32)) for i in range(8)]
        print('SBUF remaining after alloc:', nc.sbuf_bytes_remaining)
        block = es.enter_context(nc.Block())
        B = Bld(nc, block, sems, es)

        kvB = [0, 4, 1, 5]
        nbB = [2, 3, 6, 7]

        def prolog(g):
            g.memset(onesF[:], 1.0)
            g.memset(onesS[:], 1.0)
            return g.memset(carry[0][:], 0.0)
        mk = B.emit("g", [], prolog, "G")
        B.did_w(mk, "onesF", "onesS", "carA")
        for dst, src, nm in ((sgv, sg, "sgv"), (suv, su, "suv"), (sdv, sd, "sdv"),
                             (cfb, cfm, "cfb"), (sfb, sfm, "sfb"), (mcb, mc, "mcb")):
            mk = B.emit("g", [], (lambda d_, s_: (lambda g: g.dma_start(out=d_[:], in_=s_)))(dst, src), "DC")
            B.did_w(mk, nm)
        for c in (0, 1):
            mk = B.emit("g", [],
                        lambda g, c_=c: g.dma_start(
                            out=xnb[c_][:, 0:NT * RK].rearrange("p (t r) -> p t r", t=NT),
                            in_=x[:, c_ * RK:(c_ + 1) * RK].rearrange("(t p) r -> p t r", p=128)),
                        "DX%d" % c)
            B.did_w(mk, *["xn%d_t%d" % (c, t) for t in range(NT)])

        def dma_wtile(wsrc, mt, bufs, bname, sem_base, nring=2):
            sl = mt % nring
            nm = bname + str(sl)
            mk = B.emit("g", B.dep_w(nm),
                        lambda g, w_=wsrc, m_=mt, d_=bufs[sl]: g.dma_start(
                            out=d_[:], in_=w_[m_].rearrange("p a b -> p (a b)")),
                        "%s%d" % (sem_base, sl % 2))
            B.did_w(mk, nm)

        def dma_wd_piece(idx):
            ot, grp, sl = idx % 16, idx // 16, idx % 4
            nm = "wb%d" % sl
            mk = B.emit("g", B.dep_w(nm),
                        lambda g, o_=ot, p_=grp, d_=wbuf[sl]: g.dma_start(
                            out=d_[:], in_=wd[o_, :, p_ * 16:(p_ + 1) * 16, :].rearrange("p a b -> p (a b)")),
                        "DD%d" % (sl % 2))
            B.did_w(mk, nm)

        def ln_chunk(c):
            xname = ["xn%d_t%d" % (c, t) for t in range(NT)]
            ps, pq = (2, 3) if c == 0 else (6, 7)
            psn, pqn = "P%d" % ps, "P%d" % pq

            def mm_sum(t, c_=c):
                r = None
                for kt in range(NT):
                    r = t.matmul(P[ps][:], onesF[:], xnb[c_][:, kt * RK:(kt + 1) * RK],
                                 start=(kt == 0), stop=(kt == NT - 1))
                return r
            mk = B.emit("t", B.dep_r("onesF", *xname) + B.dep_w(psn), mm_sum, "T")
            B.did_w(mk, psn); B.did_r(mk, "onesF", *xname)
            for kt in range(NT):
                p = kt % 2
                mk = B.emit("v", B.dep_r(xname[kt]) + B.dep_w("sq%d" % p),
                            lambda v, k_=kt, p_=p, c_=c: v.tensor_tensor(
                                sqr[:, p_ * RK:(p_ + 1) * RK], xnb[c_][:, k_ * RK:(k_ + 1) * RK],
                                xnb[c_][:, k_ * RK:(k_ + 1) * RK], op=ALU.mult), "V")
                B.did_w(mk, "sq%d" % p); B.did_r(mk, xname[kt])
                mk2 = B.emit("t", B.dep_r("sq%d" % p, "onesF") + (B.dep_w(pqn) if kt == 0 else []),
                             lambda t, k_=kt, p_=p: t.matmul(P[pq][:], onesF[:], sqr[:, p_ * RK:(p_ + 1) * RK],
                                                             start=(k_ == 0), stop=(k_ == NT - 1)), "T")
                B.did_r(mk2, "sq%d" % p, "onesF")
                if kt == NT - 1:
                    B.did_w(mk2, pqn)
            mk = B.emit("v", B.dep_r(psn) + B.dep_w("mb%d" % c),
                        lambda v, c_=c: v.tensor_scalar_mul(mbuf[c_][:], P[ps][:], 1.0 / D), "V")
            B.did_w(mk, "mb%d" % c); B.did_r(mk, psn)
            mk = B.emit("v", B.dep_r(pqn) + B.dep_w("vv%d" % c),
                        lambda v, c_=c: v.tensor_scalar_mul(vvb[c_][:], P[pq][:], 1.0 / D), "V")
            B.did_w(mk, "vv%d" % c); B.did_r(mk, pqn)
            mk = B.emit("v", B.dep_r("mb%d" % c) + B.dep_w("mtmp"),
                        lambda v, c_=c: v.tensor_tensor(mtmp[:], mbuf[c_][:], mbuf[c_][:], op=ALU.mult), "V")
            B.did_w(mk, "mtmp"); B.did_r(mk, "mb%d" % c)
            mk = B.emit("v", B.dep_r("mtmp") + B.dep_w("vv%d" % c),
                        lambda v, c_=c: v.tensor_tensor(vvb[c_][:], vvb[c_][:], mtmp[:], op=ALU.subtract), "V")
            B.did_r(mk, "mtmp")
            mk = B.emit("v", [mk],
                        lambda v, c_=c: v.tensor_scalar_add(vvb[c_][:], vvb[c_][:], EPS), "V")
            B.did_w(mk, "vv%d" % c)
            mk = B.emit("s", B.dep_r("vv%d" % c) + B.dep_w("mtmp"),
                        lambda s, c_=c: s.activation(mtmp[:], vvb[c_][:], AF.Sqrt), "S")
            B.did_w(mk, "mtmp"); B.did_r(mk, "vv%d" % c)
            mk = B.emit("v", B.dep_r("mtmp") + B.dep_w("rsd%d" % c),
                        lambda v, c_=c: v.reciprocal(rsdb[c_][:], mtmp[:]), "V")
            B.did_w(mk, "rsd%d" % c); B.did_r(mk, "mtmp")
            for kt in range(NT):
                mk = B.emit("v", B.dep_w(xname[kt]) + B.dep_r("mb%d" % c),
                            lambda v, k_=kt, c_=c: v.tensor_tensor(
                                xnb[c_][:, k_ * RK:(k_ + 1) * RK], xnb[c_][:, k_ * RK:(k_ + 1) * RK],
                                mbuf[c_][:], op=ALU.subtract), "V")
                B.did_r(mk, "mb%d" % c)
                mk = B.emit("v", [mk] + B.dep_r("rsd%d" % c),
                            lambda v, k_=kt, c_=c: v.tensor_tensor(
                                xnb[c_][:, k_ * RK:(k_ + 1) * RK], xnb[c_][:, k_ * RK:(k_ + 1) * RK],
                                rsdb[c_][:], op=ALU.mult), "V")
                B.did_w(mk, xname[kt]); B.did_r(mk, "rsd%d" % c)

        def kv_phase():
            xnames = [["xn%d_t%d" % (c, t) for t in range(NT)] for c in (0, 1)]
            for mt in range(4):
                dma_wtile(wkv, mt, wbuf, "wb", "DW", nring=4)
            for mt in range(NT):
                par = mt % 2
                wsl = mt % 4
                for c in (0, 1):
                    s = 2 * c + par
                    pb, pbn = P[kvB[s]], "P%d" % kvB[s]
                    kvn = "kv%d_t%d" % (c, mt)

                    def mm(t, p_=pb, w_=wbuf[wsl], c_=c):
                        r = None
                        for kt in range(NT):
                            r = t.matmul(p_[:], w_[:, kt * 128:(kt + 1) * 128],
                                         xnb[c_][:, kt * RK:(kt + 1) * RK],
                                         start=(kt == 0), stop=(kt == NT - 1))
                        return r
                    mk = B.emit("t", B.dep_r("wb%d" % wsl, *xnames[c]) + B.dep_w(pbn), mm, "T")
                    B.did_w(mk, pbn); B.did_r(mk, "wb%d" % wsl, *xnames[c])
                    mk = B.emit("s", B.dep_r(pbn) + B.dep_w(kvn),
                                lambda s_, m_=mt, p_=pb, c_=c: s_.copy(
                                    kvb[c_][:, m_ * RK:(m_ + 1) * RK], p_[:]), "S")
                    B.did_r(mk, pbn); B.did_w(mk, kvn)
                for c in (0, 1):
                    s = 2 * c + par
                    nb, nbn = P[nbB[s]], "P%d" % nbB[s]
                    kvn = "kv%d_t%d" % (c, mt)
                    j = c
                    mk = B.emit("v", B.dep_r(kvn) + B.dep_w("sq%d" % j),
                                lambda v, m_=mt, j_=j, c_=c: v.tensor_tensor(
                                    sqr[:, j_ * RK:(j_ + 1) * RK], kvb[c_][:, m_ * RK:(m_ + 1) * RK],
                                    kvb[c_][:, m_ * RK:(m_ + 1) * RK], op=ALU.mult), "V")
                    B.did_w(mk, "sq%d" % j); B.did_r(mk, kvn)
                    mk = B.emit("t", B.dep_r("sq%d" % j, "onesF") + B.dep_w(nbn),
                                lambda t, j_=j, n_=nb: t.matmul(
                                    n_[:], onesF[:], sqr[:, j_ * RK:(j_ + 1) * RK],
                                    start=True, stop=True), "T")
                    B.did_w(mk, nbn); B.did_r(mk, "sq%d" % j, "onesF")
                    mk = B.emit("s", B.dep_r(nbn) + B.dep_w("rcp%d" % j),
                                lambda s_, j_=j, n_=nb: s_.activation(
                                    rcp[:, j_ * RK:(j_ + 1) * RK], n_[:], AF.Sqrt, scale=C_KV), "S")
                    B.did_w(mk, "rcp%d" % j); B.did_r(mk, nbn)
                    mk = B.emit("v", B.dep_w("rcp%d" % j),
                                lambda v, j_=j: v.reciprocal(
                                    rcp[:, j_ * RK:(j_ + 1) * RK], rcp[:, j_ * RK:(j_ + 1) * RK]), "V")
                    B.did_w(mk, "rcp%d" % j)
                    mk = B.emit("v", B.dep_w(kvn) + B.dep_r("rcp%d" % j),
                                lambda v, m_=mt, j_=j, c_=c: v.tensor_tensor(
                                    kvb[c_][:, m_ * RK:(m_ + 1) * RK], kvb[c_][:, m_ * RK:(m_ + 1) * RK],
                                    rcp[:, j_ * RK:(j_ + 1) * RK], op=ALU.mult), "V")
                    B.did_w(mk, kvn); B.did_r(mk, "rcp%d" % j)
                    mk = B.emit("v", B.dep_r(kvn) + B.dep_w("tot%d" % c),
                                lambda v, m_=mt, c_=c: v.tensor_reduce(
                                    tots[c_][:, m_:m_ + 1], kvb[c_][:, m_ * RK:(m_ + 1) * RK],
                                    mybir.AxisListType.X, ALU.add), "V")
                    B.did_r(mk, kvn)
                    if mt == NT - 1:
                        B.did_w(mk, "tot%d" % c)
                    else:
                        B.did_w(mk, "totpart%d" % c)
                if mt + 4 < NT:
                    dma_wtile(wkv, mt + 4, wbuf, "wb", "DW", nring=4)

        def carry_exchange():
            mk = B.emit("v", B.dep_r("tot0", "tot1") + B.dep_w("ctot"),
                        lambda v: v.tensor_tensor(ctot[:], tots[0][:], tots[1][:], op=ALU.add), "V")
            B.did_w(mk, "ctot"); B.did_r(mk, "tot0", "tot1")
            mk = B.emit("q", B.dep_r("ctot") + B.dep_w("cc_in"),
                        lambda q: q.dma_start(out=cc_in[:, :], in_=ctot[:]), "DI")
            B.did_r(mk, "ctot"); B.did_w(mk, "cc_in")
            mk = B.emit("g", B.dep_r("cc_in") + B.dep_w("cc_out"),
                        lambda g: g.collective_compute(
                            "AllGather", ALU.bypass,
                            replica_groups=[list(range(8))],
                            ins=[cc_in.ap().opt()], outs=[cc_out.ap().opt()]), "CC")
            B.did_r(mk, "cc_in"); B.did_w(mk, "cc_out")
            mk = B.emit("q", B.dep_r("cc_out") + B.dep_w("gth"),
                        lambda q: q.dma_start(
                            out=gth[:].rearrange("p (r t) -> p r t", r=8),
                            in_=cc_out.ap().rearrange("(r p) t -> p r t", p=128)), "DH")
            B.did_w(mk, "gth"); B.did_r(mk, "cc_out")

        def carry_combine():
            for r in range(8):
                mk = B.emit("v", B.dep_r("gth", "mcb") + B.dep_w("carA"),
                            lambda v, r_=r: v.scalar_tensor_tensor(
                                carry[0][:], gth[:, r_ * 16:(r_ + 1) * 16], mcb[:, r_:r_ + 1],
                                carry[0][:], op0=ALU.mult, op1=ALU.add), "V")
                B.did_w(mk, "carA"); B.did_r(mk, "gth", "mcb")
            mk = B.emit("v", B.dep_r("carA", "tot0") + B.dep_w("car1"),
                        lambda v: v.tensor_tensor(carry[1][:], carry[0][:], tots[0][:], op=ALU.add), "V")
            B.did_w(mk, "car1"); B.did_r(mk, "carA", "tot0")

        def scan_step(i, steps):
            mt, c = steps[i]
            kvn = "kv%d_t%d" % (c, mt)
            cn = "carA" if c == 0 else "car1"
            mk = B.emit("v", B.dep_w(kvn) + B.dep_r(cn, "onesS"),
                        lambda v, m_=mt, c_=c: v.tensor_tensor_scan(
                            kvb[c_][:, m_ * RK:(m_ + 1) * RK], onesS[:],
                            kvb[c_][:, m_ * RK:(m_ + 1) * RK],
                            carry[c_][:, m_:m_ + 1],
                            op0=ALU.mult, op1=ALU.add), "V")
            B.did_w(mk, kvn); B.did_r(mk, cn, "onesS")

        def q_dft_phase():
            xnames = [["xn%d_t%d" % (c, t) for t in range(NT)] for c in (0, 1)]
            steps = [(mt, c) for mt in range(NT) for c in (0, 1)]
            DELAY = 3

            def qproj_mm(i):
                mt, c = steps[i]
                par = mt % 2
                wsl = mt % 4
                s = 2 * c + par
                pb, pbn = P[kvB[s]], "P%d" % kvB[s]
                j = i % 4
                jr = i % 2
                qn = "qr%d" % j

                def mm(t, p_=pb, w_=wbuf[wsl], c_=c):
                    r = None
                    for kt in range(NT):
                        r = t.matmul(p_[:], w_[:, kt * 128:(kt + 1) * 128],
                                     xnb[c_][:, kt * RK:(kt + 1) * RK],
                                     start=(kt == 0), stop=(kt == NT - 1))
                    return r
                mk = B.emit("t", B.dep_r("wb%d" % wsl, *xnames[c]) + B.dep_w(pbn), mm, "T")
                B.did_w(mk, pbn); B.did_r(mk, "wb%d" % wsl, *xnames[c])
                mk = B.emit("s", B.dep_r(pbn) + B.dep_w(qn),
                            lambda s_, j_=j, p_=pb: s_.copy(qr[:, j_ * RK:(j_ + 1) * RK], p_[:]), "S")
                B.did_r(mk, pbn); B.did_w(mk, qn)
                mk = B.emit("v", B.dep_r(qn) + B.dep_w("sq%d" % jr),
                            lambda v, j_=j, r_=jr: v.tensor_tensor(
                                sqr[:, r_ * RK:(r_ + 1) * RK], qr[:, j_ * RK:(j_ + 1) * RK],
                                qr[:, j_ * RK:(j_ + 1) * RK], op=ALU.mult), "V")
                B.did_w(mk, "sq%d" % jr); B.did_r(mk, qn)

            def qproj_norm(i):
                mt, c = steps[i]
                par = mt % 2
                s = 2 * c + par
                nb, nbn = P[nbB[s]], "P%d" % nbB[s]
                j = i % 4
                jr = i % 2
                qn = "qr%d" % j
                mk = B.emit("t", B.dep_r("sq%d" % jr, "onesF") + B.dep_w(nbn),
                            lambda t, r_=jr, n_=nb: t.matmul(
                                n_[:], onesF[:], sqr[:, r_ * RK:(r_ + 1) * RK],
                                start=True, stop=True), "T")
                B.did_w(mk, nbn); B.did_r(mk, "sq%d" % jr, "onesF")
                mk = B.emit("s", B.dep_r(nbn) + B.dep_w("rcp%d" % jr),
                            lambda s_, r_=jr, n_=nb: s_.activation(
                                rcp[:, r_ * RK:(r_ + 1) * RK], n_[:], AF.Sqrt, scale=C_Q), "S")
                B.did_w(mk, "rcp%d" % jr); B.did_r(mk, nbn)
                mk = B.emit("v", B.dep_w("rcp%d" % jr),
                            lambda v, r_=jr: v.reciprocal(
                                rcp[:, r_ * RK:(r_ + 1) * RK], rcp[:, r_ * RK:(r_ + 1) * RK]), "V")
                B.did_w(mk, "rcp%d" % jr)
                mk = B.emit("v", B.dep_w(qn) + B.dep_r("rcp%d" % jr),
                            lambda v, j_=j, r_=jr: v.tensor_tensor(
                                qr[:, j_ * RK:(j_ + 1) * RK], qr[:, j_ * RK:(j_ + 1) * RK],
                                rcp[:, r_ * RK:(r_ + 1) * RK], op=ALU.mult), "V")
                B.did_w(mk, qn); B.did_r(mk, "rcp%d" % jr)

            def dft(i):
                mt, c = steps[i]
                par = mt % 2
                s = 2 * c + par
                pb, pbn = P[kvB[s]], "P%d" % kvB[s]
                nb, nbn = P[nbB[s]], "P%d" % nbB[s]
                j = i % 4
                hp = i % 2
                qn = "qr%d" % j
                kvn = "kv%d_t%d" % (c, mt)
                names = [f"hb{hp}_{k}" for k in range(6)]
                qsl = qr[:, j * RK:(j + 1) * RK]
                ksl = kvb[c][:, mt * RK:(mt + 1) * RK]
                for (src, srcn, mat, matn, dsti) in (
                        (qsl, qn, cfb, "cfb", 0),
                        (qsl, qn, sfb, "sfb", 1),
                        (ksl, kvn, cfb, "cfb", 2),
                        (ksl, kvn, sfb, "sfb", 3)):
                    pp, pn = (pb, pbn) if dsti % 2 == 0 else (nb, nbn)
                    mk = B.emit("t", B.dep_r(srcn, matn) + B.dep_w(pn),
                                lambda t, s_=src, w_=mat, p_=pp: t.matmul(
                                    p_[:], w_[:], s_, start=True, stop=True), "T")
                    B.did_w(mk, pn); B.did_r(mk, srcn, matn)
                    ek = "v" if dsti % 2 == 0 else "s"
                    if ek == "v":
                        mk = B.emit("v", B.dep_r(pn) + B.dep_w(names[dsti]),
                                    lambda v, b_=hb[hp][dsti], p_=pp: v.tensor_copy(b_[:], p_[:]), "V")
                    else:
                        mk = B.emit("s", B.dep_r(pn) + B.dep_w(names[dsti]),
                                    lambda s_, b_=hb[hp][dsti], p_=pp: s_.copy(b_[:], p_[:]), "S")
                    B.did_r(mk, pn); B.did_w(mk, names[dsti])

                def prods(v, b_=hb[hp]):
                    QC, QS, KC, KS, T5, T6 = b_
                    v.tensor_tensor(T5[:], QC[:], KC[:], op=ALU.mult)
                    v.tensor_tensor(T6[:], QS[:], KS[:], op=ALU.mult)
                    v.tensor_tensor(T5[:], T5[:], T6[:], op=ALU.subtract)
                    v.tensor_tensor(T6[:], QC[:], KS[:], op=ALU.mult)
                    v.tensor_tensor(QC[:], QS[:], KC[:], op=ALU.mult)
                    return v.tensor_tensor(T6[:], T6[:], QC[:], op=ALU.add)
                mk = B.emit("v", B.dep_r(*names) + B.dep_w(names[0], names[4], names[5]), prods, "V")
                B.did_r(mk, *names); B.did_w(mk, names[0], names[4], names[5])
                mk = B.emit("t", B.dep_r(names[4], names[5], "cfb", "sfb") + B.dep_w(pbn),
                            lambda t, b_=hb[hp], p_=pb: (
                                t.matmul(p_[:], cfb[:], b_[4][:], start=True, stop=False),
                                t.matmul(p_[:], sfb[:], b_[5][:], start=False, stop=True))[-1], "T")
                B.did_w(mk, pbn); B.did_r(mk, names[4], names[5], "cfb", "sfb")
                vhn = "vh%d_t%d" % (c, mt)
                mk = B.emit("s", B.dep_r(pbn) + B.dep_w(kvn, vhn),
                            lambda s_, m_=mt, c_=c, p_=pb: s_.copy(
                                kvb[c_][:, m_ * RK:(m_ + 1) * RK], p_[:]), "S")
                B.did_r(mk, pbn); B.did_w(mk, kvn); B.did_w(mk, vhn)

            for i in range(len(steps)):
                qproj_mm(i)
                mt, c = steps[i]
                if c == 1 and mt + 4 < NT:
                    dma_wtile(wq, mt + 4, wbuf, "wb", "DW", nring=4)
                if i == 2:
                    carry_combine()
                    for k in range(3):
                        scan_step(k, steps)
                elif i >= 3:
                    scan_step(i, steps)
                if i >= 1:
                    qproj_norm(i - 1)
                if i >= DELAY:
                    dft(i - DELAY)
            qproj_norm(len(steps) - 1)
            for i in range(len(steps) - DELAY, len(steps)):
                dft(i)

        def wo_phase():
            woB = {(0, 0): 6, (0, 1): 7, (1, 0): 2, (1, 1): 3}
            for ot in range(4):
                dma_wtile(wo, ot, wbuf, "wb", "DW", nring=4)
            for ot in range(NT):
                par = ot % 2
                wsl = ot % 4
                for c in (0, 1):
                    vh_names = ["vh%d_t%d" % (c, k) for k in range(NT)]
                    pb, pbn = P[woB[(c, par)]], "P%d" % woB[(c, par)]

                    def mmo(t, p_=pb, w_=wbuf[wsl], c_=c):
                        r = None
                        for kt in range(NT):
                            r = t.matmul(p_[:], w_[:, kt * 128:(kt + 1) * 128],
                                         kvb[c_][:, kt * RK:(kt + 1) * RK],
                                         start=(kt == 0), stop=(kt == NT - 1))
                        return r
                    mk = B.emit("t", B.dep_r("wb%d" % wsl, *vh_names) + B.dep_w(pbn), mmo, "T")
                    B.did_w(mk, pbn); B.did_r(mk, "wb%d" % wsl, *vh_names)
                    an = "ast%d" % c
                    mk = B.emit("s", B.dep_r(pbn) + B.dep_w(an),
                                lambda s_, p_=pb, a_=ast[c]: s_.copy(a_[:], p_[:]), "S")
                    B.did_r(mk, pbn); B.did_w(mk, an)
                    mk = B.emit("q", B.dep_r(an) + B.dep_w("ascr%d_t%d" % (c, ot)),
                                lambda q, o_=ot, c_=c, a_=ast[c]: q.dma_start(
                                    out=ascr[o_ * 128:(o_ + 1) * 128, c_ * RK:(c_ + 1) * RK], in_=a_[:]),
                                "DA%d" % c)
                    B.did_r(mk, an); B.did_w(mk, "ascr%d_t%d" % (c, ot))
                if ot + 4 < NT:
                    dma_wtile(wo, ot + 4, wbuf, "wb", "DW", nring=4)

        def f_chunk(c):
            xnames = ["xn%d_t%d" % (c, t) for t in range(NT)]
            outcol = c * RK
            dma_wtile(wg, 0, wgb, "wgb", "DG")
            dma_wtile(wu, 0, wub, "wub", "DU")
            dma_wtile(wg, 1, wgb, "wgb", "DG")
            dma_wtile(wu, 1, wub, "wub", "DU")
            for grp in range(4):
                for i in range(16):
                    mt = grp * 16 + i
                    par = mt % 2
                    pg, pgn = P[par], "P%d" % par
                    pu, pun = P[2 + par], "P%d" % (2 + par)

                    def mmg(t, p_=pg, w_=wgb[par], c_=c):
                        r = None
                        for kt in range(NT):
                            r = t.matmul(p_[:], w_[:, kt * 128:(kt + 1) * 128],
                                         xnb[c_][:, kt * RK:(kt + 1) * RK],
                                         start=(kt == 0), stop=(kt == NT - 1))
                        return r
                    mk = B.emit("t", B.dep_r("wgb%d" % par, *xnames) + B.dep_w(pgn), mmg, "T")
                    B.did_w(mk, pgn); B.did_r(mk, "wgb%d" % par, *xnames)
                    if mt + 2 < 64:
                        dma_wtile(wg, mt + 2, wgb, "wgb", "DG")

                    def mmu(t, p_=pu, w_=wub[par], c_=c):
                        r = None
                        for kt in range(NT):
                            r = t.matmul(p_[:], w_[:, kt * 128:(kt + 1) * 128],
                                         xnb[c_][:, kt * RK:(kt + 1) * RK],
                                         start=(kt == 0), stop=(kt == NT - 1))
                        return r
                    mk = B.emit("t", B.dep_r("wub%d" % par, *xnames) + B.dep_w(pun), mmu, "T")
                    B.did_w(mk, pun); B.did_r(mk, "wub%d" % par, *xnames)
                    if mt + 2 < 64:
                        dma_wtile(wu, mt + 2, wub, "wub", "DU")
                    mk = B.emit("s", B.dep_r(pgn, "sgv") + B.dep_w("silg%d" % par),
                                lambda s, m_=mt, p_=pg, g_=silg[par]: s.activation(
                                    g_[:], p_[:], AF.Silu, scale=sgv[:, m_:m_ + 1]), "S")
                    B.did_r(mk, pgn, "sgv"); B.did_w(mk, "silg%d" % par)
                    mk = B.emit("v", B.dep_r(pun, "suv", "silg%d" % par) + B.dep_w("fuse_t%d" % i),
                                lambda v, m_=mt, i_=i, p_=pu, g_=silg[par]: v.scalar_tensor_tensor(
                                    fuse[:, i_ * RK:(i_ + 1) * RK], p_[:], suv[:, m_:m_ + 1], g_[:],
                                    op0=ALU.mult, op1=ALU.mult), "V")
                    B.did_r(mk, pun, "suv", "silg%d" % par); B.did_w(mk, "fuse_t%d" % i)
                if grp == 0:
                    for k in range(4):
                        dma_wd_piece(k)
                for ot in range(NT):
                    idx = grp * 16 + ot
                    dpar = ot % 2
                    wsl = idx % 4
                    pd, pdn = P[4 + dpar], "P%d" % (4 + dpar)
                    fdeps = [B.dep_r("fuse_t%d" % i)[0] for i in range(16)]
                    deps = B.dep_r("wb%d" % wsl) + fdeps + B.dep_w(pdn)

                    def mmd(t, p_=pd, w_=wbuf[wsl]):
                        r = None
                        for i in range(16):
                            r = t.matmul(p_[:], w_[:, i * 128:(i + 1) * 128],
                                         fuse[:, i * RK:(i + 1) * RK],
                                         start=(i == 0), stop=(i == 15))
                        return r
                    mk = B.emit("t", deps, mmd, "T")
                    B.did_w(mk, pdn)
                    B.did_r(mk, "wb%d" % wsl, *["fuse_t%d" % i for i in range(16)])
                    if idx + 4 < 64:
                        dma_wd_piece(idx + 4)
                    if grp == 0:
                        mk = B.emit("s", B.dep_r(pdn) + B.dep_w("acc_t%d" % ot),
                                    lambda s, o_=ot, p_=pd: s.copy(acc[:, o_ * RK:(o_ + 1) * RK], p_[:]), "S")
                    else:
                        mk = B.emit("v", B.dep_r(pdn, "acc_t%d" % ot),
                                    lambda v, o_=ot, p_=pd: v.tensor_tensor(
                                        acc[:, o_ * RK:(o_ + 1) * RK], acc[:, o_ * RK:(o_ + 1) * RK],
                                        p_[:], op=ALU.add), "V")
                    B.did_r(mk, pdn); B.did_w(mk, "acc_t%d" % ot)
            for ot in range(NT):
                par = ot % 2
                mk = B.emit("q", B.dep_w("ard%d" % par) + B.dep_r("ascr%d_t%d" % (c, ot)),
                            lambda q, o_=ot, c_=c, a_=a_rd[par]: q.dma_start(
                                out=a_[:], in_=ascr[o_ * 128:(o_ + 1) * 128, c_ * RK:(c_ + 1) * RK]), "DB%d" % par)
                B.did_w(mk, "ard%d" % par); B.did_r(mk, "ascr%d_t%d" % (c, ot))
                mk = B.emit("g", B.dep_w("xrd%d" % par),
                            lambda g, o_=ot, x_=x_rd[par]: g.dma_start(
                                out=x_[:], in_=x[o_ * 128:(o_ + 1) * 128, outcol:outcol + RK]),
                            "DB%d" % par)
                B.did_w(mk, "xrd%d" % par)
                mk = B.emit("v", B.dep_r("ard%d" % par, "xrd%d" % par) + B.dep_w("axr%d" % par),
                            lambda v, a_=a_rd[par], x_=x_rd[par], r_=axr[par]: v.tensor_tensor(
                                r_[:], a_[:], x_[:], op=ALU.add), "V")
                B.did_r(mk, "ard%d" % par, "xrd%d" % par); B.did_w(mk, "axr%d" % par)
                mk = B.emit("v", B.dep_r("acc_t%d" % ot, "sdv", "axr%d" % par) + B.dep_w("ost%d" % par),
                            lambda v, o_=ot, r_=axr[par], u_=outst[par]: v.scalar_tensor_tensor(
                                u_[:], acc[:, o_ * RK:(o_ + 1) * RK], sdv[:, o_:o_ + 1], r_[:],
                                op0=ALU.mult, op1=ALU.add), "V")
                B.did_r(mk, "acc_t%d" % ot, "sdv", "axr%d" % par); B.did_w(mk, "ost%d" % par)
                mk = B.emit("q", B.dep_r("ost%d" % par),
                            lambda q, o_=ot, u_=outst[par]: q.dma_start(
                                out=out[o_ * 128:(o_ + 1) * 128, outcol:outcol + RK], in_=u_[:]), "DO")
                B.did_r(mk, "ost%d" % par)

        ln_chunk(0)
        ln_chunk(1)
        kv_phase()
        for mt in range(4):
            dma_wtile(wq, mt, wbuf, "wb", "DW", nring=4)
        carry_exchange()
        q_dft_phase()
        wo_phase()
        f_chunk(0)
        f_chunk(1)
        B.emit("q", [("DO", B.cnt["DO"])], lambda q: q.nop(), "G")
        B.flush()
    return nc


_NC_CACHE = None


def kernel(L0_attn_x, L0_attn_W_q, L0_attn_W_kv, L0_attn_W_o,
           L0_ffn_w_gate, L0_ffn_w_up, L0_ffn_w_down,
           L0_ffn_s_gate, L0_ffn_s_up, L0_ffn_s_down):
    x = np.asarray(L0_attn_x, dtype=np.float32)
    Wq = np.asarray(L0_attn_W_q, dtype=np.float32)
    Wkv = np.asarray(L0_attn_W_kv, dtype=np.float32)
    Wo = np.asarray(L0_attn_W_o, dtype=np.float32)
    wg = np.asarray(L0_ffn_w_gate, dtype=np.int8)
    wu = np.asarray(L0_ffn_w_up, dtype=np.int8)
    wd = np.asarray(L0_ffn_w_down, dtype=np.int8)
    sg = np.asarray(L0_ffn_s_gate, dtype=np.float32)
    su = np.asarray(L0_ffn_s_up, dtype=np.float32)
    sdn = np.asarray(L0_ffn_s_down, dtype=np.float32)

    t4a = lambda W: np.ascontiguousarray(W.reshape(16, 128, 16, 128).transpose(0, 3, 2, 1))
    wq_t, wkv_t, wo_t = t4a(Wq), t4a(Wkv), t4a(Wo)
    wg_t = np.ascontiguousarray(wg.reshape(16, 128, 64, 128).transpose(2, 1, 0, 3))
    wu_t = np.ascontiguousarray(wu.reshape(16, 128, 64, 128).transpose(2, 1, 0, 3))
    wd_t = np.ascontiguousarray(wd.reshape(64, 128, 16, 128).transpose(2, 1, 0, 3))
    sg_t = np.ascontiguousarray(sg.reshape(64, 128).T)
    su_t = np.ascontiguousarray(su.reshape(64, 128).T)
    sd_t = np.ascontiguousarray(sdn.reshape(16, 128).T)
    n = np.arange(128)
    ang = 2.0 * np.pi * np.outer(n, n) / 128.0
    cf = np.cos(ang).astype(np.float32)
    sf = np.sin(ang).astype(np.float32)

    in_maps = []
    for c in range(8):
        b, j = c // 4, c % 4
        x_c = np.ascontiguousarray(x[b, j * 1024:(j + 1) * 1024, :].T)
        mrow = np.zeros((8,), dtype=np.float32)
        for r in range(8):
            if r // 4 == b and r % 4 < j:
                mrow[r] = 1.0
        mcb = np.ascontiguousarray(np.tile(mrow, (128, 1)))
        in_maps.append({
            "x": x_c, "wq": wq_t, "wkv": wkv_t, "wo": wo_t,
            "wg": wg_t, "wu": wu_t, "wd": wd_t,
            "sg": sg_t, "su": su_t, "sd": sd_t,
            "cf": cf, "sf": sf, "mc": mcb,
        })

    global _NC_CACHE
    if _NC_CACHE is None:
        _NC_CACHE = build_program()
    res = run_bass_kernel_spmd(_NC_CACHE, in_maps, core_ids=list(range(8)))
    outp = np.empty((2, 4096, D), dtype=np.float32)
    for c in range(8):
        b, j = c // 4, c % 4
        outp[b, j * 1024:(j + 1) * 1024, :] = np.asarray(res.results[c]["out"]).T
    return outp
